# version 03
# baseline (speedup 1.0000x reference)
import numpy as np
import concourse.bass as bass
import concourse.mybir as mybir
from concourse.bass_utils import run_bass_kernel_spmd

F32 = mybir.dt.float32
BF16 = mybir.dt.bfloat16
AF = mybir.ActivationFunctionType
ALU = mybir.AluOpType
AX = mybir.AxisListType

D = 1024
NDC = 8
NH = 16
HD = 64
I_DIM = 2048
NIC = 16
NE = 8
EPS = 1e-5


class Ctr:
    def __init__(self, nc, name):
        self.h = nc.alloc_semaphore(name)
        self.v = 0
        self.name = name


class Prog:
    ENGINES = ("sync", "gpsimd", "tensor", "vector", "scalar")

    def __init__(self, nc):
        self.nc = nc
        self.ops = {e: [] for e in self.ENGINES}

    def ctr(self, name):
        return Ctr(self.nc, name)

    def add(self, eng, emit, waits=(), ctr=None, n=1):
        self.ops[eng].append((emit, [w for w in waits if w is not None and w[1] is not None and w[1] > 0],
                              ctr, n))
        if ctr is None:
            return None
        ctr.v += n
        return ctr.v

    def dma(self, eng, out, in_, waits=(), ctr=None, **kw):
        return self.add(eng, lambda e, o=out, i=in_, kw=kw: e.dma_start(o, i, **kw),
                        waits, ctr, 16)

    def emit_all(self):
        with self.nc.Block() as block:
            for eng in self.ENGINES:
                ops = self.ops[eng]

                def runner(engobj, ops=ops):
                    last_wait = {}
                    for emit, waits, ctr, n in ops:
                        for (c, v) in waits:
                            if last_wait.get(id(c), 0) >= v:
                                continue
                            engobj.wait_ge(c.h, v)
                            last_wait[id(c)] = v
                        instr = emit(engobj)
                        if ctr is not None:
                            instr.then_inc(ctr.h, n)
                getattr(block, eng)(runner)


def build_program(T=4096, S=2048, dbg=()):
    nseq = T // S
    STC = S // 128
    NQT = S // 512
    NKC = S // 128
    TH = T // 2
    NHALF = 2
    HTC = TH // 128
    NTT = TH // 512

    nc = bass.Bass("TRN2", target_bir_lowering=False)
    p = Prog(nc)

    IN = lambda n, s: nc.dram_tensor(n, s, F32, kind="ExternalInput").ap()
    x0_h = IN("x0", [T, D])
    decay_h = IN("decay", [1, 1])
    b_h = IN("b_row", [1, D]); gamma_h = IN("gamma_row", [1, D]); beta_h = IN("beta_row", [1, D])
    ln1w_h = IN("ln1w_row", [1, D]); ln2w_h = IN("ln2w_row", [1, D])
    wT_h = IN("wT", [D, D])
    wqT_h = IN("wqT", [D, D]); wkT_h = IN("wkT", [D, D])
    wvT_h = IN("wvT", [D, D]); woT_h = IN("woT", [D, D])
    gateT_h = IN("gateT", [D, NE])
    w1T_h = IN("w1T", [NE, D, I_DIM]); w3T_h = IN("w3T", [NE, D, I_DIM])
    w2T_h = IN("w2T", [NE, I_DIM, D])
    out_h = nc.dram_tensor("out", [T, D], F32, kind="ExternalOutput").ap()

    def scratch(name, shape, dt):
        kind = "ExternalOutput" if name in dbg else "Internal"
        return nc.dram_tensor(name, shape, dt, kind=kind).ap()

    x1_h = scratch("x1_s", [T, D], F32)
    z_h = scratch("z_s", [T, D], F32)
    f_h = scratch("f_s", [T, D], F32)
    qt_h = scratch("qt_s", [nseq, D, S], BF16)
    kt_h = scratch("kt_s", [nseq, D, S], BF16)
    v_h = scratch("v_s", [T, D], BF16)
    aot_h = scratch("aot_s", [nseq, D, S], BF16)
    r2t_h = scratch("r2t_s", [NDC, 128, T], BF16)
    we_h = scratch("we_s", [T // 128, 128, NE], F32)
    pow_h = scratch("pow_s", [1, 256], F32)
    yln_h = scratch("yln_s", [T, D], BF16)
    st_h = scratch("st_s", [T, D], BF16)
    yb_h = scratch("yb_s", [T, D], F32)

    sb = lambda n, s, d_: nc.alloc_sbuf_tensor(n, s, d_)
    b_b = sb("b_b", [128, D], F32)
    gam_b = sb("gam_b", [128, D], F32)
    bet_b = sb("bet_b", [128, D], F32)
    l1w_b = sb("l1w_b", [128, D], F32)
    l2w_b = sb("l2w_b", [128, D], F32)
    eps_c = sb("eps_c", [128, 1], F32)
    ones_bf = sb("ones_bf", [128, 128], BF16)
    ones_f = sb("ones_f", [128, 128], F32)
    ident_bf = sb("ident_bf", [128, 128], BF16)
    ident_f = sb("ident_f", [128, 128], F32)
    powx = sb("powx", [1, 256], F32)
    a_sc = sb("a_sc", [1, 1], F32)
    apow_b = sb("apow_b", [128, 128], F32)
    a128_c = sb("a128_c", [128, 1], F32)
    ltri = sb("ltri", [128, 128], BF16)
    gate_w = sb("gate_w", [128, NDC * NE], F32)
    we_all = sb("we_all", [128, (T // 128) * NE], F32)
    stat = sb("stat", [128, 96], F32)

    ra0, _ = nc.bump_sbuf(65536)
    AT = nc.alloc_sbuf_tensor_at
    wq_w = AT("wq_w", [128, NDC * D], BF16, offset=ra0)
    wk_w = AT("wk_w", [128, NDC * D], BF16, offset=ra0 + 16384)
    wv_w = AT("wv_w", [128, NDC * D], BF16, offset=ra0 + 32768)
    wo_w = AT("wo_w", [128, NDC * D], BF16, offset=ra0)
    wt_w = AT("wt_w", [128, NDC * D], BF16, offset=ra0 + 49152)
    g_t = AT("g_t", [128, NIC * TH], BF16, offset=ra0)
    rb0, _ = nc.bump_sbuf(NDC * S * 2)
    r1t_s = AT("r1t_s", [128, NDC * S], BF16, offset=rb0)
    r2in = AT("r2in", [128, NDC * TH], BF16, offset=rb0)
    rc0, _ = nc.bump_sbuf(53248)
    off = rc0
    def at(name, shape, dt):
        nonlocal off
        t = AT(name, shape, dt, offset=off)
        nbytes = int(np.prod(shape[1:])) * (2 if dt == BF16 else 4)
        off += (nbytes + 31) & ~31
        return t
    x0_t = at("x0_t", [128, 2 * D], F32)
    yln_t = at("yln_t", [128, D], BF16)
    st_t = at("st_t", [128, 2 * D], BF16)
    carry = at("carry", [128, 2 * NDC], F32)
    ybuf = at("ybuf", [128, D], F32)
    gbuf = at("gbuf", [128, D], F32)
    sqs = at("sqs", [128, D], F32)
    x1b = at("x1b", [128, 2 * D], F32)
    zb = at("zb", [128, 2 * D], F32)
    r1tmp = at("r1tmp", [128, D], F32)
    r1b = at("r1b", [128, D], BF16)
    assert off - rc0 <= 53248, off - rc0
    off = rc0
    aoin = at("aoin", [128, 2 * NDC * 128], BF16)
    zin = at("zin", [128, 2 * D], F32)
    hb = at("hb", [128, D], F32)
    r2tmp = at("r2tmp", [128, D], F32)
    r2f = at("r2f", [128, D], F32)
    r2tf = at("r2tf", [128, NDC * 128], F32)
    r2tb = at("r2tb", [128, NDC * 256], BF16)
    logit = at("logit", [128, NE], F32)
    gs1 = at("gs1", [128, NE], F32)
    gs2 = at("gs2", [128, NE], F32)
    sq4 = at("sq4", [128, D], F32)
    assert off - rc0 <= 53248, off - rc0
    off = rc0
    w2s = at("w2s", [128, NIC * D], BF16)
    w1s = at("w1s", [128, 2 * NDC * 128], BF16)
    w3s = at("w3s", [128, 2 * NDC * 128], BF16)
    silu_t = at("silu_t", [128, 2 * 512], BF16)
    ysc = at("ysc", [128, 2 * D], F32)
    assert off - rc0 <= 53248, off - rc0
    rd0, _ = nc.bump_sbuf(35712)
    off = rd0
    ybufB = at("ybufB", [128, D], F32)
    gbufB = at("gbufB", [128, D], F32)
    sqsB = at("sqsB", [128, D], F32)
    r1tmpB = at("r1tmpB", [128, D], F32)
    r1bB = at("r1bB", [128, D], BF16)
    ylnB = at("ylnB", [128, D], BF16)
    assert off - rd0 <= 35712, off - rd0
    off = rd0
    pstage = at("pstage", [128, 2 * 512], BF16)
    vstage = at("vstage", [128, 2 * 512], BF16)
    assert off - rd0 <= 35712, off - rd0
    off = rd0
    qt2 = at("qt2", [128, 2 * S], BF16)
    kt2 = at("kt2", [128, 2 * S], BF16)
    vh2 = at("vh2", [128, 2 * NKC * 130], BF16)
    mcols = at("mcols", [128, 32], F32)
    mrow_t = at("mrow_t", [16, 128], BF16)
    negm = at("negm", [1, 2048], BF16)
    pt_t = at("pt_t", [128, 2 * 512], BF16)
    recb = at("recb", [128, 512], F32)
    lrow = AT("lrow", [128, 512], F32, offset=off - 2048)
    ao_st = at("ao_st", [128, 512], BF16)
    assert off - rd0 <= 35712, off - rd0
    off = rd0
    fin = at("fin", [128, 2 * D], F32)
    x1in = at("x1in", [128, 2 * D], F32)
    oub = at("oub", [128, 2 * D], F32)
    sq7 = at("sq7", [128, D], F32)
    assert off - rd0 <= 35712, off - rd0

    psA = nc.alloc_psum_tensor("psA", [128, 2048], F32)
    psB = nc.alloc_psum_tensor("psB", [128, 1024], F32)
    psC = nc.alloc_psum_tensor("psC", [128, 512], F32)
    psD = nc.alloc_psum_tensor("psD", [128, 1024], BF16)

    c_ld = p.ctr("c_ld")
    c_st = p.ctr("c_st")
    c_gw = p.ctr("c_gw")
    c_ga = p.ctr("c_ga")
    c_pe = p.ctr("c_pe")
    c_dv = p.ctr("c_dv")
    c_ac = p.ctr("c_ac")

    SY, GP, TE, VE, SC = "sync", "gpsimd", "tensor", "vector", "scalar"


    def ln_stats(src, n_feat, cols, waits, sq_scratch):
        sq_scratch = sq_scratch[:]
        s0, s1, mu, var, rstd, nmr = (stat[:, cols + i:cols + i + 1] for i in range(6))
        t0 = p.add(VE, lambda e, s=src, o=s0: e.reduce_sum(o, s, axis=AX.X), waits, c_dv)
        t_sq = p.add(SC, lambda e, s=src, o=sq_scratch, a=s1: e.activation(
            o, s, AF.Square, accum_out=a), waits, c_ac)
        t1 = p.add(VE, lambda e, i=s0, o=mu: e.tensor_scalar(o, i, 1.0 / n_feat, None, op0=ALU.mult),
                   [(c_ac, t_sq), (c_dv, t0)], c_dv)
        t2 = p.add(VE, lambda e, i=mu, o=nmr: e.tensor_scalar(o, i, i, None, op0=ALU.mult),
                   [(c_dv, t1)], c_dv)
        t_var = p.add(VE, lambda e, i=s1, o=var, m=nmr: e.tensor_scalar(
            o, i, 1.0 / n_feat, m, op0=ALU.mult, op1=ALU.subtract), [(c_dv, t2)], c_dv)
        t_ln = p.add(SC, lambda e, i=var, o=rstd: e.activation(o, i, AF.Ln, bias=eps_c[:, 0:1]),
                     [(c_dv, t_var)], c_ac)
        t_ex = p.add(SC, lambda e, i=rstd, o=rstd: e.activation(o, i, AF.Exp, scale=-0.5),
                     [(c_ac, t_ln)], c_ac)
        t_nmr = p.add(VE, lambda e, i=mu, r=rstd, o=nmr: e.scalar_tensor_tensor(
            o, i, -1.0, r, op0=ALU.mult, op1=ALU.mult), [(c_ac, t_ex), (c_dv, t_var)], c_dv)
        return t_nmr, rstd, nmr

    def rms_stats(src, n_feat, cols, waits, sq_scratch):
        sq_scratch = sq_scratch[:]
        s1, msq, rr = (stat[:, cols + i:cols + i + 1] for i in range(3))
        t_sq = p.add(SC, lambda e, s=src, o=sq_scratch, a=s1: e.activation(
            o, s, AF.Square, accum_out=a), waits, c_ac)
        t_m = p.add(VE, lambda e, i=s1, o=msq: e.tensor_scalar(o, i, 1.0 / n_feat, None,
                                                               op0=ALU.mult),
                    [(c_ac, t_sq)], c_dv)
        t_ln = p.add(SC, lambda e, i=msq, o=rr: e.activation(o, i, AF.Ln, bias=eps_c[:, 0:1]),
                     [(c_dv, t_m)], c_ac)
        t_rr = p.add(SC, lambda e, i=rr, o=rr: e.activation(o, i, AF.Exp, scale=-0.5),
                     [(c_ac, t_ln)], c_ac)
        return t_rr, rr

    for dst, srch in ((b_b, b_h), (gam_b, gamma_h), (bet_b, beta_h),
                      (l1w_b, ln1w_h), (l2w_b, ln2w_h)):
        p.dma(SY, dst[:], bass.AP(srch.tensor, 0, [[0, 128], [1, D]]), (), c_ld)
    t_ld_a = p.dma(SY, a_sc[:], decay_h, (), c_ld)

    for dc in range(NDC):
        p.dma(GP, wt_w[:, dc * D:(dc + 1) * D], wT_h[dc * 128:(dc + 1) * 128, :], (), c_gw)
    for dc in range(NDC):
        t_gw_wt = p.dma(GP, gate_w[:, dc * NE:(dc + 1) * NE],
                        gateT_h[dc * 128:(dc + 1) * 128, :], (), c_gw)

    p.add(GP, lambda e: e.memset(ident_bf[:], 0.0))
    p.add(GP, lambda e: e.affine_select(out=ident_bf[:], in_=ident_bf[:],
                                        compare_op=ALU.not_equal, fill=1.0, base=0,
                                        pattern=[[-1, 128]], channel_multiplier=1))
    p.add(GP, lambda e: e.memset(ident_f[:], 0.0))
    p.add(GP, lambda e: e.affine_select(out=ident_f[:], in_=ident_f[:],
                                        compare_op=ALU.not_equal, fill=1.0, base=0,
                                        pattern=[[-1, 128]], channel_multiplier=1))
    p.add(GP, lambda e: e.memset(ones_bf[:], 1.0))
    t_gp_done = p.add(GP, lambda e: e.memset(ones_f[:], 1.0), (), c_gw)

    t_a = p.add(SC, lambda e: e.activation(a_sc[:], a_sc[:], AF.Sigmoid),
                [(c_ld, t_ld_a)], c_ac)
    p.add(VE, lambda e: e.memset(eps_c[:], EPS))

    p.add(VE, lambda e: e.memset(powx[:], 0.0), [(c_ac, t_a)])
    t_ms1 = p.add(VE, lambda e: e.memset(powx[0:1, 127:128], 1.0), (), c_dv)
    t_pow = t_ms1
    t_pow = None
    for k in range(1, 129):
        t_pow = p.add(VE, lambda e, k=k: e.tensor_scalar(
            powx[0:1, 127 + k:128 + k], powx[0:1, 126 + k:127 + k], a_sc[0:1, 0:1],
            None, op0=ALU.mult), [(c_dv, t_pow)], c_dv)
    t_pwst = p.dma(SY, pow_h, powx[0:1, :], [(c_dv, t_pow)], c_st)
    p.dma(SY, apow_b[:], bass.AP(pow_h.tensor, 128, [[0, 128], [1, 128]]),
          [(c_st, t_pwst)], c_ld)
    t_a128 = p.dma(SY, a128_c[:], bass.AP(pow_h.tensor, 255, [[0, 128], [1, 1]]), (), c_ld)
    t_ltri = None
    for srow in range(128):
        t_ltri = p.dma(GP, ltri[srow:srow + 1, :],
                       bass.AP(pow_h.tensor, 127 - srow, [[1, 1], [1, 128]]),
                       [(c_st, t_pwst)] if srow == 0 else (), c_gw)

    tok_x1_st = [0] * (T // 128)
    tok_z_st = [0] * (T // 128)
    tok_fh_st = [0] * (T // 128)
    tok_r2_st = {}
    tok_we = [0] * (T // 128)
    p4_last_pe = 0
    p4_last_r2st = 0

    for seq in range(nseq):
        w_wait = [(c_pe, p4_last_pe)] if seq > 0 else []
        t_gw_qkv = None
        for w_sbuf, w_ap in ((wq_w, wqT_h), (wk_w, wkT_h), (wv_w, wvT_h)):
            for dc in range(NDC):
                t_gw_qkv = p.dma(GP, w_sbuf[:, dc * D:(dc + 1) * D],
                                 w_ap[dc * 128:(dc + 1) * 128, :],
                                 w_wait if dc == 0 else (), c_gw)

        p1_w = [(c_pe, c_pe.v), (c_dv, c_dv.v), (c_ac, c_ac.v), (c_st, c_st.v)]
        t_carry0 = p.add(VE, lambda e: e.memset(carry[:, NDC:2 * NDC], 0.0),
                         [(c_ld, t_a128), (c_gw, t_ltri)], c_dv)

        dbg_tok = {}
        t_dv_evac = {-1: t_carry0}
        t_pe_scan = {-1: 0, -2: 0}
        t_dv_yb = {-1: 0}
        t_dv_x1add = {-1: 0, -2: 0}
        t_pe_r1t = {-1: 0, -2: 0}
        t_dv_r1t = {-1: 0, -2: 0}
        t_ac_r1s = {-1: 0}

        for tc in range(STC):
            pb = tc % 2
            gtc = seq * STC + tc
            r0 = gtc * 128
            ybuf_p = ybuf if pb == 0 else ybufB
            gbuf_p = gbuf if pb == 0 else gbufB
            sqs_p = sqs if pb == 0 else sqsB
            r1tmp_p = r1tmp if pb == 0 else r1tmpB
            r1b_p = r1b if pb == 0 else r1bB
            yln_p = yln_t if pb == 0 else ylnB
            xo = x0_t[:, pb * D:(pb + 1) * D]
            t_ldx = p.dma(SY, xo, x0_h[r0:r0 + 128, :],
                          [(c_dv, t_dv_x1add[tc - 2])] + (p1_w if tc < 2 else []), c_ld)
            lw = [(c_ld, t_ldx)]
            t_nmr, rstd, nmr = ln_stats(xo, D, 0 + 8 * pb, lw, sqs_p)
            t_yln = p.add(SC, lambda e, x=xo, r=rstd, m=nmr, o=yln_p: e.activation(
                o[:], x, AF.Identity, scale=r, bias=m),
                [(c_dv, t_nmr), (c_pe, t_pe_scan[tc - 2]),
                 (c_st, dbg_tok.get(("yln", tc - 2), 0))], c_ac)
            if "yln_s" in dbg:
                dbg_tok[("yln", tc)] = p.dma(SY, yln_h[r0:r0 + 128, :], yln_t[:],
                                             [(c_ac, t_yln)], c_st)
            t_scan = None
            for dc in range(NDC):
                t_scan = p.add(TE, lambda e, dc=dc, o=yln_p: e.matmul(
                    psA[:, dc * 128:(dc + 1) * 128], o[:, dc * 128:(dc + 1) * 128],
                    ltri[:], start=True, stop=True),
                    [(c_ac, t_yln), (c_dv, t_dv_evac[tc - 1]), (c_gw, t_ltri)]
                    if dc == 0 else (), c_pe if dc == NDC - 1 else None)
            t_pe_scan[tc] = t_scan
            po, pn = (1 - pb) * NDC, pb * NDC
            t_ev = None
            for dc in range(NDC):
                p.add(VE, lambda e, dc=dc, po=po, pn=pn: e.scalar_tensor_tensor(
                    carry[:, pn + dc:pn + dc + 1], a128_c[:], carry[:, po + dc:po + dc + 1],
                    psA[:, dc * 128 + 127:dc * 128 + 128], op0=ALU.mult, op1=ALU.add),
                    [(c_pe, t_scan)] if dc == 0 else ())
                t_ev = p.add(VE, lambda e, dc=dc, po=po, pb=pb: e.scalar_tensor_tensor(
                    st_t[:, pb * D + dc * 128:pb * D + (dc + 1) * 128], apow_b[:],
                    carry[:, po + dc:po + dc + 1], psA[:, dc * 128:(dc + 1) * 128],
                    op0=ALU.mult, op1=ALU.add),
                    [(c_st, dbg_tok.get(("st", tc - 2), 0))] if dc == 0 else (),
                    c_dv if dc == NDC - 1 else None)
            t_dv_evac[tc] = t_ev
            if "st_s" in dbg:
                dbg_tok[("st", tc)] = p.dma(SY, st_h[r0:r0 + 128, :],
                                            st_t[:, pb * D:(pb + 1) * D],
                                            [(c_dv, t_ev)], c_st)
            t_y = None
            for et in range(2):
                for dc in range(NDC):
                    t_y = p.add(TE, lambda e, et=et, dc=dc, pb=pb: e.matmul(
                        psA[:, 1024 + et * 512:1024 + (et + 1) * 512],
                        st_t[:, pb * D + dc * 128:pb * D + (dc + 1) * 128],
                        wt_w[:, dc * D + et * 512:dc * D + (et + 1) * 512],
                        start=(dc == 0), stop=(dc == NDC - 1)),
                        [(c_dv, t_ev), (c_dv, t_dv_yb[tc - 1]), (c_gw, t_gw_wt)]
                        if (et == 0 and dc == 0) else (),
                        c_pe if (et == 1 and dc == NDC - 1) else None)
            t_yb = p.add(VE, lambda e, o=ybuf_p: e.tensor_add(o[:], psA[:, 1024:2048], b_b[:]),
                         [(c_pe, t_y)], c_dv)
            if "yb_s" in dbg:
                dbg_tok[("yb", tc)] = p.dma(SY, yb_h[r0:r0 + 128, :], ybuf_p[:],
                                            [(c_dv, t_yb)], c_st)
            t_dv_yb[tc] = t_yb
            t_nmr1, rstd1, nmr1 = ln_stats(ybuf_p[:], D, 16 + 8 * pb, [(c_dv, t_yb)], sqs_p)
            t_l1 = p.add(SC, lambda e, r=rstd1, m=nmr1, i=ybuf_p, o=sqs_p: e.activation(
                o[:], i[:], AF.Identity, scale=r, bias=m), [(c_dv, t_nmr1)], c_ac)
            t_g1 = p.add(VE, lambda e, i=sqs_p, o=gbuf_p: e.tensor_mul(o[:], i[:], gam_b[:]),
                         [(c_ac, t_l1)], c_dv)
            t_g2 = p.add(VE, lambda e, o=gbuf_p: e.tensor_add(o[:], o[:], bet_b[:]),
                         [(c_dv, t_g1)], c_dv)
            t_nmr2, rstd2, nmr2 = ln_stats(gbuf_p[:], D, 32 + 8 * pb, [(c_dv, t_g2)], sqs_p)
            t_l2 = p.add(SC, lambda e, r=rstd2, m=nmr2, i=gbuf_p, o=ybuf_p: e.activation(
                o[:], i[:], AF.Identity, scale=r, bias=m),
                [(c_dv, t_nmr2), (c_st, dbg_tok.get(("yb", tc), 0))], c_ac)
            t_x1a = p.add(VE, lambda e, x=xo, pb=pb, i=ybuf_p: e.tensor_add(
                x1b[:, pb * D:(pb + 1) * D], x, i[:]),
                [(c_ac, t_l2), (c_st, tok_x1_st[gtc - 2] if gtc >= 2 else 0)], c_dv)
            t_dv_x1add[tc] = t_x1a
            tok_x1_st[gtc] = p.dma(SY, x1_h[r0:r0 + 128, :], x1b[:, pb * D:(pb + 1) * D],
                                   [(c_dv, t_x1a)], c_st)
            t_nmrz, rstdz, nmrz = ln_stats(x1b[:, pb * D:(pb + 1) * D], D, 48 + 8 * pb,
                                           [(c_dv, t_x1a)], sqs)
            t_z = p.add(SC, lambda e, r=rstdz, m=nmrz, pb=pb: e.activation(
                zb[:, pb * D:(pb + 1) * D], x1b[:, pb * D:(pb + 1) * D], AF.Identity,
                scale=r, bias=m),
                [(c_dv, t_nmrz), (c_st, tok_z_st[gtc - 2] if gtc >= 2 else 0)], c_ac)
            tok_z_st[gtc] = p.dma(SY, z_h[r0:r0 + 128, :], zb[:, pb * D:(pb + 1) * D],
                                  [(c_ac, t_z)], c_st)
            t_rr1, rr1 = rms_stats(zb[:, pb * D:(pb + 1) * D], D, 64 + 4 * pb,
                                   [(c_ac, t_z)], sqs)
            t_r1s = p.add(SC, lambda e, rr=rr1, pb=pb, o=r1tmp_p: e.activation(
                o[:], zb[:, pb * D:(pb + 1) * D], AF.Identity, scale=rr),
                [(c_ac, t_rr1), (c_dv, t_dv_r1t[tc - 2])], c_ac)
            t_ac_r1s[tc] = t_r1s
            t_r1 = p.add(VE, lambda e, i=r1tmp_p, o=r1b_p: e.tensor_mul(o[:], i[:], l1w_b[:]),
                         [(c_ac, t_r1s), (c_pe, t_pe_r1t[tc - 2])], c_dv)
            t_tr = None
            for dc in range(NDC):
                t_tr = p.add(TE, lambda e, dc=dc, i=r1b_p: e.transpose(
                    psD[:, dc * 128:(dc + 1) * 128], i[:, dc * 128:(dc + 1) * 128],
                    ident_bf[:]),
                    [(c_dv, t_r1), (c_dv, t_dv_r1t[tc - 1]), (c_gw, t_gp_done)]
                    if dc == 0 else (), c_pe if dc == NDC - 1 else None)
            t_pe_r1t[tc] = t_tr
            t_r1t = None
            for dc in range(NDC):
                t_r1t = p.add(VE, lambda e, dc=dc, tc=tc: e.tensor_copy(
                    r1t_s[:, dc * S + tc * 128:dc * S + (tc + 1) * 128],
                    psD[:, dc * 128:(dc + 1) * 128]),
                    [(c_pe, t_tr)] if dc == 0 else (), c_dv if dc == NDC - 1 else None)
            t_dv_r1t[tc] = t_r1t

        p1_last_dv = t_dv_r1t[STC - 1]

        t_p2_dv = {-1: 0, -2: 0}
        t_p2_st = {-1: 0, -2: 0}
        unit = 0
        t_mm = None
        for w_sbuf, dst_h in ((wq_w, qt_h), (wk_w, kt_h)):
            for ec in range(NDC):
                for tt4 in range(NQT):
                    pp = unit % 2
                    for dc in range(NDC):
                        t_mm = p.add(TE, lambda e, dc=dc, ec=ec, tt4=tt4, pp=pp, w=w_sbuf: e.matmul(
                            psB[:, pp * 512:(pp + 1) * 512],
                            w[:, dc * D + ec * 128:dc * D + ec * 128 + 128],
                            r1t_s[:, dc * S + tt4 * 512:dc * S + (tt4 + 1) * 512],
                            start=(dc == 0), stop=(dc == NDC - 1)),
                            [(c_dv, p1_last_dv), (c_gw, t_gw_qkv),
                             (c_dv, t_p2_dv[unit - 2])] if dc == 0 else (),
                            c_pe if dc == NDC - 1 else None)
                    t_ev2 = p.add(VE, lambda e, pp=pp: e.tensor_copy(
                        pstage[:, pp * 512:(pp + 1) * 512], psB[:, pp * 512:(pp + 1) * 512]),
                        [(c_pe, t_mm), (c_st, t_p2_st[unit - 2])], c_dv)
                    t_p2_dv[unit] = t_ev2
                    t_p2_st[unit] = p.dma(
                        SY, dst_h[seq, ec * 128:(ec + 1) * 128, tt4 * 512:(tt4 + 1) * 512],
                        pstage[:, pp * 512:(pp + 1) * 512], [(c_dv, t_ev2)], c_st)
                    unit += 1
        last_qk_st = t_p2_st[unit - 1]
        vunit = 0
        t_v_dv = {-1: 0, -2: 0}
        t_v_st = {-1: 0, -2: 0}
        for tcq in range(STC):
            for et in range(2):
                pp = vunit % 2
                for dc in range(NDC):
                    t_mm = p.add(TE, lambda e, dc=dc, tcq=tcq, et=et, pp=pp: e.matmul(
                        psA[:, pp * 512:(pp + 1) * 512],
                        r1t_s[:, dc * S + tcq * 128:dc * S + (tcq + 1) * 128],
                        wv_w[:, dc * D + et * 512:dc * D + (et + 1) * 512],
                        start=(dc == 0), stop=(dc == NDC - 1)),
                        [(c_dv, t_v_dv[vunit - 2]), (c_dv, p1_last_dv)] if dc == 0 else (),
                        c_pe if dc == NDC - 1 else None)
                t_ev3 = p.add(VE, lambda e, pp=pp: e.tensor_copy(
                    vstage[:, pp * 512:(pp + 1) * 512], psA[:, pp * 512:(pp + 1) * 512]),
                    [(c_pe, t_mm), (c_st, t_v_st[vunit - 2])], c_dv)
                t_v_dv[vunit] = t_ev3
                t_v_st[vunit] = p.dma(
                    SY, v_h[seq * S + tcq * 128:seq * S + (tcq + 1) * 128,
                            et * 512:(et + 1) * 512],
                    vstage[:, pp * 512:(pp + 1) * 512], [(c_dv, t_ev3)], c_st)
                vunit += 1
        p2_last_pe = t_mm
        last_v_st = t_v_st[vunit - 1]
        t_gw_wo = None
        for dc in range(NDC):
            t_gw_wo = p.dma(GP, wo_w[:, dc * D:(dc + 1) * D],
                            woT_h[dc * 128:(dc + 1) * 128, :],
                            [(c_pe, p2_last_pe)] if dc == 0 else (), c_gw)

        p3_dv0 = c_dv.v
        qc_glob = [0]
        t_p1max = {-1: p3_dv0, -2: p3_dv0}
        t_pair_pe_done = {-1: 0, -2: 0}
        t_ao_dv = {}
        t_recb_dv = {}
        t_st_exp = {-1: 0, -2: 0}
        t_trm_prev_dv = p3_dv0
        t_trm_tok = {-1: 0, -2: 0}
        t_aost = {-1: 0}
        pair_tok = {}
        p1_last_tok = {}
        aou = 0
        t_pv = None

        def emit_pair_loads(hp):
            pqb = hp % 2
            wv_ = [(c_pe, t_pair_pe_done[hp - 2]), (c_st, last_qk_st), (c_st, last_v_st)]
            t_lq = p.dma(SY, qt2[:, pqb * S:(pqb + 1) * S],
                         qt_h[seq, hp * 128:(hp + 1) * 128, :], wv_, c_ld)
            t_lk = p.dma(SY, kt2[:, pqb * S:(pqb + 1) * S],
                         kt_h[seq, hp * 128:(hp + 1) * 128, :], (), c_ld)
            t_lv = None
            for sub in range(2):
                h2 = 2 * hp + sub
                hoff = 0 if sub == 0 else 65
                srcv = bass.AP(v_h.tensor, (seq * S) * D + h2 * HD,
                               [[D, 128], [128 * D, NKC], [1, HD]])
                dstv = vh2[:, pqb * NKC * 130:(pqb + 1) * NKC * 130].rearrange(
                    "q (kc w) -> q kc w", kc=NKC)[:, :, hoff:hoff + HD]
                t_lv = p.dma(SY, dstv, srcv, [(c_gw, t_ones3)], c_ld)
            pair_tok[hp] = (t_lq, t_lk, t_lv)

        def make_p1(h2):
            hp = h2 // 2
            pqb = hp % 2
            hsl = slice(0, 64) if h2 % 2 == 0 else slice(64, 128)
            mbase = (h2 % 2) * 16
            state = {"i": 0}

            def emit_unit():
                i = state["i"]; state["i"] += 1
                qc, kh = i // 2, i % 2
                qg = qc_glob[0]; qc_glob[0] += 1
                pp = qg % 2
                t_s1 = None
                for kn in range(2):
                    xw = ()
                    if kn == 0:
                        xw = [(c_ld, pair_tok[hp][1]), (c_dv, t_p1max[qg - 2])]
                        if i == 0:
                            xw = xw + [(c_pe, t_trm_tok[h2 - 2])]
                    t_s1 = p.add(TE, lambda e, qc=qc, kh=kh, kn=kn, pp=pp,
                                 pqb=pqb, hsl=hsl: e.matmul(
                        psA[:, pp * 1024 + kn * 512:pp * 1024 + (kn + 1) * 512],
                        qt2[hsl, pqb * S + qc * 128:pqb * S + (qc + 1) * 128],
                        kt2[hsl, pqb * S + kh * 1024 + kn * 512:
                            pqb * S + kh * 1024 + (kn + 1) * 512],
                        start=True, stop=True), xw, c_pe if kn == 1 else None)
                if kh == 0:
                    t_mx = p.add(VE, lambda e, qc=qc, pp=pp, mbase=mbase: e.reduce_max(
                        mcols[:, mbase + qc:mbase + qc + 1],
                        psA[:, pp * 1024:(pp + 1) * 1024],
                        axis=AX.X, negate=True), [(c_pe, t_s1)], c_dv)
                else:
                    t_tmp = p.add(VE, lambda e, pp=pp: e.reduce_max(
                        stat[:, 90:91], psA[:, pp * 1024:(pp + 1) * 1024],
                        axis=AX.X, negate=True), [(c_pe, t_s1)], c_dv)
                    t_mx = p.add(VE, lambda e, qc=qc, mbase=mbase: e.tensor_tensor(
                        mcols[:, mbase + qc:mbase + qc + 1],
                        mcols[:, mbase + qc:mbase + qc + 1], stat[:, 90:91],
                        op=ALU.min), [(c_dv, t_tmp)], c_dv)
                t_p1max[qg] = t_mx
                p1_last_tok[h2] = t_mx
            return emit_unit, state

        t_ones3 = None
        for pbv in range(2):
            for kcv in range(NKC):
                for hoffv in (64, 129):
                    t_ones3 = p.add(GP, lambda e, c=pbv * (NKC * 130) + kcv * 130 + hoffv:
                                    e.memset(vh2[:, c:c + 1], 1.0),
                                    [(c_dv, p1_last_dv)] if t_ones3 is None else (), c_gw)
        emit_pair_loads(0)
        p1_em = {0: make_p1(0)}
        for _ in range(2 * STC):
            p1_em[0][0]()

        for h_ in range(NH):
            hp = h_ // 2
            pqb = hp % 2
            hsl = slice(0, 64) if h_ % 2 == 0 else slice(64, 128)
            hoff = 0 if h_ % 2 == 0 else 65
            if h_ % 2 == 0 and hp + 1 < NH // 2:
                emit_pair_loads(hp + 1)
            if h_ + 1 < NH:
                p1_em[h_ + 1] = make_p1(h_ + 1)
            mbase = (h_ % 2) * 16
            t_trm = p.add(TE, lambda e, mbase=mbase: e.transpose(
                psC[0:16, 0:128], mcols[:, mbase:mbase + 16], ident_f[:]),
                [(c_dv, p1_last_tok[h_]), (c_dv, t_trm_prev_dv),
                 (c_dv, t_ao_dv.get((h_ - 1, NQT - 1), p3_dv0))], c_pe)
            t_trm_tok[h_] = t_trm
            t_mr = p.add(VE, lambda e: e.tensor_copy(mrow_t[:], psC[0:16, 0:128]),
                         [(c_pe, t_trm)], c_dv)
            t_trm_prev_dv = t_mr
            t_nm = p.dma(SY, negm[0:1, :], mrow_t[:, :], [(c_dv, t_mr)], c_ld)
            for qt in range(NQT):
                pend_pv = []

                def emit_pv(kc, pp2, first_w=()):
                    nonlocal t_pv
                    pvw = [(c_ac, t_st_exp[(h_ * NQT + qt) * NKC + kc])] + list(first_w)
                    t_pv = p.add(TE, lambda e, kc=kc, pqb=pqb, hoff=hoff, pp2=pp2: e.matmul(
                        psC[0:65, 0:512],
                        vh2[:, pqb * NKC * 130 + kc * 130 + hoff:
                            pqb * NKC * 130 + kc * 130 + hoff + 65],
                        pt_t[:, pp2 * 512:(pp2 + 1) * 512],
                        start=(kc == 0), stop=(kc == NKC - 1)),
                        pvw, c_pe if kc == NKC - 1 else None)

                for kc in range(NKC):
                    stu = (h_ * NQT + qt) * NKC + kc
                    pp2 = stu % 2
                    p.add(TE, lambda e, kc=kc, qt=qt, pp2=pp2, pqb=pqb, hsl=hsl: e.matmul(
                        psB[:, pp2 * 512:(pp2 + 1) * 512],
                        kt2[hsl, pqb * S + kc * 128:pqb * S + (kc + 1) * 128],
                        qt2[hsl, pqb * S + qt * 512:pqb * S + (qt + 1) * 512],
                        start=True, stop=False),
                        [(c_ld, t_nm), (c_ac, t_st_exp[stu - 2]),
                         (c_dv, t_recb_dv.get((h_, qt - 1),
                                              t_recb_dv.get((h_ - 1, NQT - 1), p3_dv0)))])
                    t_s2 = p.add(TE, lambda e, qt=qt, pp2=pp2: e.matmul(
                        psB[:, pp2 * 512:(pp2 + 1) * 512], ones_bf[0:1, :],
                        negm[0:1, qt * 512:(qt + 1) * 512], start=False, stop=True),
                        (), c_pe)
                    t_ex = p.add(SC, lambda e, pp2=pp2: e.activation(
                        pt_t[:, pp2 * 512:(pp2 + 1) * 512],
                        psB[:, pp2 * 512:(pp2 + 1) * 512], AF.Exp, scale=0.125),
                        [(c_pe, t_s2)], c_ac)
                    t_st_exp[stu] = t_ex
                    if kc % 2 == 1 and h_ + 1 < NH and p1_em[h_ + 1][1]["i"] < 2 * STC:
                        p1_em[h_ + 1][0]()
                    if kc > 0:
                        fw = ()
                        if kc == 1:
                            fw = [(c_ld, pair_tok[hp][2]), (c_dv, t_mr),
                                  (c_dv, t_ao_dv.get((h_, qt - 1),
                                                     t_ao_dv.get((h_ - 1, NQT - 1), p3_dv0)))]
                            emit_pv(0, (h_ * NQT + qt) * NKC % 2, fw)
                        else:
                            emit_pv(kc - 1, ((h_ * NQT + qt) * NKC + kc - 1) % 2)
                emit_pv(NKC - 1, ((h_ * NQT + qt) * NKC + NKC - 1) % 2)
                t_lc = p.add(VE, lambda e: e.tensor_copy(lrow[64:65, :], psC[64:65, 0:512]),
                             [(c_pe, t_pv)], c_dv)
                t_ln2 = p.add(SC, lambda e: e.activation(lrow[64:65, :], lrow[64:65, :],
                                                         AF.Ln), [(c_dv, t_lc)], c_ac)
                t_rc = p.add(SC, lambda e: e.activation(lrow[64:65, :], lrow[64:65, :],
                                                        AF.Exp, scale=-1.0),
                             [(c_ac, t_ln2)], c_ac)
                t_bc = p.add(TE, lambda e: e.matmul(
                    psB[0:64, 0:512], ones_f[64:65, 0:64], lrow[64:65, :],
                    start=True, stop=True),
                    [(c_ac, t_rc),
                     (c_ac, t_st_exp[(h_ * NQT + qt) * NKC + NKC - 1])], c_pe)
                t_rb = p.add(VE, lambda e: e.tensor_copy(recb[0:64, :], psB[0:64, 0:512]),
                             [(c_pe, t_bc)], c_dv)
                t_recb_dv[(h_, qt)] = t_rb
                t_ao = p.add(VE, lambda e: e.tensor_mul(
                    ao_st[0:64, :], psC[0:64, 0:512], recb[0:64, :]),
                    [(c_dv, t_rb), (c_st, t_aost[aou - 1])], c_dv)
                t_ao_dv[(h_, qt)] = t_ao
                t_aost[aou] = p.dma(SY, aot_h[seq, h_ * 64:(h_ + 1) * 64,
                                              qt * 512:(qt + 1) * 512],
                                    ao_st[0:64, :], [(c_dv, t_ao)], c_st)
                aou += 1
            t_pair_pe_done[hp] = t_pv
        p3_last_st = t_aost[aou - 1]

        p4_dv0 = c_dv.v
        t_p4_dv = {-1: 0, -2: 0}
        t_r2tb_dv = {-1: 0}
        t_logit_dv = {-1: p4_dv0}
        t_hb_st = {-1: 0, -2: 0}
        t_r2_prev = 0
        t_r2s_prev = 0
        t_tr4_prev = 0
        t_gmm = None
        for tc in range(STC):
            pb = tc % 2
            gtc = seq * STC + tc
            r0 = gtc * 128
            src = bass.AP(aot_h.tensor, seq * D * S + tc * 128,
                          [[S, 128], [128 * S, NDC], [1, 128]])
            dsta = aoin[:, pb * NDC * 128:(pb + 1) * NDC * 128]
            t_lao = p.dma(SY, dsta.rearrange("q (dc w) -> q dc w", dc=NDC), src,
                          [(c_st, p3_last_st), (c_dv, t_p4_dv[tc - 2])], c_ld)
            t_lz = p.dma(SY, zin[:, pb * D:(pb + 1) * D], z_h[r0:r0 + 128, :],
                         [(c_st, tok_z_st[gtc])], c_ld)
            t_wo = None
            for et in range(2):
                for dc in range(NDC):
                    t_wo = p.add(TE, lambda e, et=et, dc=dc, pb=pb: e.matmul(
                        psA[:, pb * 1024 + et * 512:pb * 1024 + (et + 1) * 512],
                        aoin[:, pb * NDC * 128 + dc * 128:pb * NDC * 128 + (dc + 1) * 128],
                        wo_w[:, dc * D + et * 512:dc * D + (et + 1) * 512],
                        start=(dc == 0), stop=(dc == NDC - 1)),
                        [(c_ld, t_lao), (c_gw, t_gw_wo), (c_dv, t_p4_dv[tc - 2]),
                         (c_dv, p4_dv0)] if (et == 0 and dc == 0) else (),
                        c_pe if (et == 1 and dc == NDC - 1) else None)
            t_h = p.add(VE, lambda e, pb=pb: e.tensor_add(
                hb[:], psA[:, pb * 1024:(pb + 1) * 1024], zin[:, pb * D:(pb + 1) * D]),
                [(c_pe, t_wo), (c_ld, t_lz), (c_st, t_hb_st[tc - 1]),
                 (c_ac, t_r2s_prev)], c_dv)
            t_p4_dv[tc] = t_h
            t_hb_st[tc] = tok_fh_st[gtc] = p.dma(SY, f_h[r0:r0 + 128, :], hb[:],
                                                 [(c_dv, t_h)], c_st)
            t_rr2, rr2 = rms_stats(hb[:], D, 72 + 4 * pb, [(c_dv, t_h)], sq4)
            t_r2s = p.add(SC, lambda e, rr=rr2: e.activation(
                r2tmp[:], hb[:], AF.Identity, scale=rr),
                [(c_ac, t_rr2), (c_dv, t_r2_prev)], c_ac)
            t_r2s_prev = t_r2s
            t_r2 = p.add(VE, lambda e: e.tensor_mul(r2f[:], r2tmp[:], l2w_b[:]),
                         [(c_ac, t_r2s), (c_pe, t_tr4_prev)], c_dv)
            t_r2_prev = t_r2
            t_tr4 = None
            for dc in range(NDC):
                t_tr4 = p.add(TE, lambda e, dc=dc: e.transpose(
                    psB[:, dc * 128:(dc + 1) * 128], r2f[:, dc * 128:(dc + 1) * 128],
                    ident_f[:]),
                    [(c_dv, t_r2), (c_dv, t_recb_dv.get((NH - 1, NQT - 1), p4_dv0))]
                    if dc == 0 else (), c_pe if dc == NDC - 1 else None)
            t_tr4_prev = t_tr4
            for dc in range(NDC):
                p.add(VE, lambda e, dc=dc: e.tensor_copy(
                    r2tf[:, dc * 128:(dc + 1) * 128], psB[:, dc * 128:(dc + 1) * 128]),
                    [(c_pe, t_tr4)] if dc == 0 else ())
            bat = tc % 2
            t_r2tb = None
            for dc in range(NDC):
                t_r2tb = p.add(VE, lambda e, dc=dc, bat=bat: e.tensor_copy(
                    r2tb[:, dc * 256 + bat * 128:dc * 256 + (bat + 1) * 128],
                    psB[:, dc * 128:(dc + 1) * 128]),
                    [(c_st, tok_r2_st.get((gtc - 2) // 2, 0))] if dc == 0 else (),
                    c_dv if dc == NDC - 1 else None)
            t_r2tb_dv[tc] = t_r2tb
            if bat == 1:
                tok_r2 = None
                for dc in range(NDC):
                    tok_r2 = p.dma(SY, r2t_h[dc, :, (gtc - 1) * 128:(gtc + 1) * 128],
                                   r2tb[:, dc * 256:(dc + 1) * 256],
                                   [(c_dv, t_r2tb)], c_st)
                tok_r2_st[gtc // 2] = tok_r2
            for dc in range(NDC):
                t_gmm = p.add(TE, lambda e, dc=dc: e.matmul(
                    psC[:, 0:NE], r2tf[:, dc * 128:(dc + 1) * 128],
                    gate_w[:, dc * NE:(dc + 1) * NE],
                    start=(dc == 0), stop=(dc == NDC - 1)),
                    [(c_dv, t_r2tb), (c_dv, t_logit_dv[tc - 1]), (c_gw, t_gw_wt)]
                    if dc == 0 else (), c_pe if dc == NDC - 1 else None)
            g0, g1, g2, g3, g4 = (stat[:, 83 + i:84 + i] for i in range(5))
            t_lg = p.add(VE, lambda e: e.tensor_copy(logit[:], psC[:, 0:NE]),
                         [(c_pe, t_gmm)], c_dv)
            t_logit_dv[tc] = t_lg
            tg = p.add(VE, lambda e: e.reduce_max(g0, logit[:], axis=AX.X, negate=True),
                       [(c_dv, t_lg)], c_dv)
            tg = p.add(VE, lambda e: e.tensor_scalar(gs1[:], logit[:], g0, None, op0=ALU.add),
                       [(c_dv, tg)], c_dv)
            tg = p.add(VE, lambda e: e.tensor_scalar(gs2[:], gs1[:], 0.0, None, op0=ALU.is_ge),
                       [(c_dv, tg)], c_dv)
            tg = p.add(VE, lambda e: e.scalar_tensor_tensor(gs1[:], gs2[:], -1e30, logit[:],
                                                            op0=ALU.mult, op1=ALU.add),
                       [(c_dv, tg)], c_dv)
            tg = p.add(VE, lambda e: e.reduce_max(g1, gs1[:], axis=AX.X, negate=True),
                       [(c_dv, tg)], c_dv)
            t_d12 = p.add(VE, lambda e: e.tensor_sub(g2, g1, g0), [(c_dv, tg)], c_dv)
            t_sg = p.add(SC, lambda e: e.activation(g3, g2, AF.Sigmoid),
                         [(c_dv, t_d12)], c_ac)
            tg = p.add(VE, lambda e: e.tensor_scalar(g4, g3, -1.0, 1.0, op0=ALU.mult, op1=ALU.add),
                       [(c_ac, t_sg)], c_dv)
            tg = p.add(VE, lambda e: e.tensor_scalar(gs1[:], gs1[:], g1, None, op0=ALU.add),
                       [(c_dv, tg)], c_dv)
            tg = p.add(VE, lambda e: e.tensor_scalar(gs1[:], gs1[:], 0.0, None, op0=ALU.is_ge),
                       [(c_dv, tg)], c_dv)
            tg = p.add(VE, lambda e: e.tensor_scalar(gs2[:], gs2[:], g3, None, op0=ALU.mult),
                       [(c_dv, tg)], c_dv)
            t_we = p.add(VE, lambda e, gtc=gtc: e.scalar_tensor_tensor(
                we_all[:, gtc * NE:(gtc + 1) * NE], gs1[:], g4, gs2[:],
                op0=ALU.mult, op1=ALU.add), [(c_dv, tg)], c_dv)
            tok_we[gtc] = t_we
            if "we_s" in dbg:
                p.dma(SY, we_h[gtc, :, :], we_all[:, gtc * NE:(gtc + 1) * NE],
                      [(c_dv, t_we)], c_st)
        p4_last_pe = t_gmm
        p4_last_r2st = tok_r2_st[(seq * STC + STC - 1) // 2]

    p6_w = [(c_pe, c_pe.v), (c_dv, c_dv.v), (c_ac, c_ac.v), (c_st, c_st.v)]
    t_uv_dv = {-1: 0, -2: 0}
    t_y_dv = {-1: 0, -2: 0}
    t_y_pe_done = {}
    t_add = {}
    t_uv_pe_tok = {}
    last_uv_pe_of_e = {}
    t_out_st = {-1: 0, -2: 0}
    t_p7_dv = {-1: 0, -2: 0}
    uvu = 0
    yu = 0

    def prev_key(half, e_):
        if e_ > 0:
            return (half, e_ - 1)
        if half > 0:
            return (half - 1, NE - 1)
        return None

    for half in range(NHALF):
        t_r2in = None
        for dc in range(NDC):
            t_r2in = p.dma(SY, r2in[:, dc * TH:(dc + 1) * TH],
                           r2t_h[dc, :, half * TH:(half + 1) * TH],
                           ([(c_st, p4_last_r2st)] + (p6_w if half == 0 else
                            [(c_pe, last_uv_pe_of_e[(half - 1, NE - 1)])]))
                           if dc == 0 else (), c_ld)

        for e_ in range(NE):
            pk = prev_key(half, e_)
            t_w2 = None
            for ic in range(NIC):
                t_w2 = p.dma(GP, w2s[:, ic * D:(ic + 1) * D],
                             w2T_h[e_, ic * 128:(ic + 1) * 128, :],
                             ((p6_w if pk is None else [(c_pe, t_y_pe_done[pk])])
                              if ic == 0 else ()), c_gw)
            t_g = None
            t_uvmm = None
            for ic in range(NIC):
                sp = ic % 2
                w1src = bass.AP(w1T_h.tensor, e_ * D * I_DIM + ic * 128,
                                [[I_DIM, 128], [128 * I_DIM, NDC], [1, 128]])
                w3src = bass.AP(w3T_h.tensor, e_ * D * I_DIM + ic * 128,
                                [[I_DIM, 128], [128 * I_DIM, NDC], [1, 128]])
                if ic < 2:
                    wsl_wait = (p6_w if pk is None else
                                [(c_pe, t_uv_pe_tok[(pk[0], pk[1], NIC - 2 + ic)])])
                else:
                    wsl_wait = [(c_pe, t_uv_pe_tok[(half, e_, ic - 2)])]
                p.dma(GP, w1s[:, sp * 1024:(sp + 1) * 1024].rearrange(
                    "q (dc w) -> q dc w", dc=NDC), w1src, wsl_wait, c_gw)
                t_w3 = p.dma(GP, w3s[:, sp * 1024:(sp + 1) * 1024].rearrange(
                    "q (dc w) -> q dc w", dc=NDC), w3src, (), c_gw)
                for nt in range(NTT):
                    pp = uvu % 2
                    ub = psB if pp == 0 else psA
                    for dc in range(NDC):
                        uw = ()
                        if dc == 0:
                            uw = [(c_gw, t_w3), (c_ld, t_r2in), (c_dv, t_uv_dv[uvu - 2])]
                            if uvu == 0:
                                uw = uw + p6_w
                        p.add(TE, lambda e, dc=dc, nt=nt, sp=sp, ub=ub: e.matmul(
                            ub[:, 0:512],
                            w1s[:, sp * 1024 + dc * 128:sp * 1024 + (dc + 1) * 128],
                            r2in[:, dc * TH + nt * 512:dc * TH + (nt + 1) * 512],
                            start=(dc == 0), stop=(dc == NDC - 1)), uw)
                    for dc in range(NDC):
                        t_uvmm = p.add(TE, lambda e, dc=dc, nt=nt, sp=sp, ub=ub: e.matmul(
                            ub[:, 512:1024],
                            w3s[:, sp * 1024 + dc * 128:sp * 1024 + (dc + 1) * 128],
                            r2in[:, dc * TH + nt * 512:dc * TH + (nt + 1) * 512],
                            start=(dc == 0), stop=(dc == NDC - 1)),
                            (), c_pe if dc == NDC - 1 else None)
                    t_si = p.add(SC, lambda e, pp=pp, ub=ub: e.activation(
                        silu_t[:, pp * 512:(pp + 1) * 512], ub[:, 0:512], AF.Silu),
                        [(c_pe, t_uvmm)], c_ac)
                    gw_ = [(c_ac, t_si)]
                    if uvu == 0:
                        gw_ += p6_w
                    elif pk is not None and ic == 0 and nt == 0:
                        gw_ += [(c_pe, t_y_pe_done[pk])]
                    t_g = p.add(VE, lambda e, pp=pp, ub=ub, ic=ic, nt=nt: e.tensor_mul(
                        g_t[:, ic * TH + nt * 512:ic * TH + (nt + 1) * 512],
                        silu_t[:, pp * 512:(pp + 1) * 512], ub[:, 512:1024]), gw_, c_dv)
                    t_uv_dv[uvu] = t_g
                    uvu += 1
                t_uv_pe_tok[(half, e_, ic)] = t_uvmm
            last_uv_pe_of_e[(half, e_)] = t_uvmm
            t_ymm = None
            for tcl in range(HTC):
                gtc = half * HTC + tcl
                for ot in range(2):
                    yp = yu % 2
                    for ic in range(NIC):
                        t_ymm = p.add(TE, lambda e, ic=ic, tcl=tcl, ot=ot, yp=yp: e.matmul(
                            psA[:, 1024 + yp * 512:1024 + (yp + 1) * 512],
                            g_t[:, ic * TH + tcl * 128:ic * TH + tcl * 128 + 128],
                            w2s[:, ic * D + ot * 512:ic * D + (ot + 1) * 512],
                            start=(ic == 0), stop=(ic == NIC - 1)),
                            [(c_dv, t_g), (c_gw, t_w2), (c_dv, t_y_dv[yu - 2])]
                            if ic == 0 else (), c_pe if ic == NIC - 1 else None)
                    if tcl >= 2:
                        slot_prev = t_add.get((half, e_, tcl - 2), 0)
                    elif pk is not None:
                        slot_prev = t_add.get((pk[0], pk[1], HTC - 2 + tcl), 0)
                    else:
                        slot_prev = 0
                    t_ye = p.add(VE, lambda e, ot=ot, yp=yp, tcl=tcl, gtc=gtc, e_=e_:
                                 e.tensor_scalar(
                        ysc[:, (tcl % 2) * D + ot * 512:(tcl % 2) * D + (ot + 1) * 512],
                        psA[:, 1024 + yp * 512:1024 + (yp + 1) * 512],
                        we_all[:, gtc * NE + e_:gtc * NE + e_ + 1], None, op0=ALU.mult),
                        [(c_pe, t_ymm), (c_dv, tok_we[gtc]), (c_ga, slot_prev)], c_dv)
                    t_y_dv[yu] = t_ye
                    yu += 1
                r0 = gtc * 128
                order_prev = t_add.get((pk[0], pk[1], tcl), 0) if pk is not None else 0
                t_ad = p.dma(GP, f_h[r0:r0 + 128, :],
                             ysc[:, (tcl % 2) * D:(tcl % 2 + 1) * D],
                             [(c_dv, t_ye), (c_st, tok_fh_st[gtc]), (c_ga, order_prev)],
                             c_ga, accum_op=ALU.add)
                t_add[(half, e_, tcl)] = t_ad
            t_y_pe_done[(half, e_)] = t_ymm

        for tcl in range(HTC):
            gtc = half * HTC + tcl
            r0 = gtc * 128
            pb = tcl % 2
            t_lf = p.dma(SY, fin[:, pb * D:(pb + 1) * D], f_h[r0:r0 + 128, :],
                         ([(c_ga, t_add[(half, NE - 1, tcl)]),
                           (c_dv, t_p7_dv[tcl - 2])] +
                          (p6_w if (half == 0 and tcl < 2) else [])), c_ld)
            t_lx = p.dma(SY, x1in[:, pb * D:(pb + 1) * D], x1_h[r0:r0 + 128, :],
                         [(c_st, tok_x1_st[gtc])], c_ld)
            t_nmrf, rstdf, nmrf = ln_stats(fin[:, pb * D:(pb + 1) * D], D, 0 + 8 * pb,
                                           [(c_ld, t_lf)], sq7)
            t_lnf = p.add(SC, lambda e, r=rstdf, m=nmrf, pb=pb: e.activation(
                oub[:, pb * D:(pb + 1) * D], fin[:, pb * D:(pb + 1) * D], AF.Identity,
                scale=r, bias=m), [(c_dv, t_nmrf), (c_st, t_out_st[tcl - 2])], c_ac)
            t_o = p.add(VE, lambda e, pb=pb: e.tensor_add(
                oub[:, pb * D:(pb + 1) * D], oub[:, pb * D:(pb + 1) * D],
                x1in[:, pb * D:(pb + 1) * D]), [(c_ac, t_lnf), (c_ld, t_lx)], c_dv)
            t_p7_dv[tcl] = t_o
            t_out_st[tcl] = p.dma(SY, out_h[r0:r0 + 128, :], oub[:, pb * D:(pb + 1) * D],
                                  [(c_dv, t_o)], c_st)

    p.add(SY, lambda e: e.nop(), [(c_st, c_st.v), (c_ga, c_ga.v), (c_ld, c_ld.v),
                                  (c_gw, c_gw.v)])
    p.emit_all()
    return nc


def host_inputs(L0_attn_Z, L0_attn_decay, L0_attn_W, L0_attn_b, L0_attn_gamma,
                L0_attn_beta, L0_ffn_ln1_w, L0_ffn_ln2_w, L0_ffn_wq, L0_ffn_wk,
                L0_ffn_wv, L0_ffn_wo, L0_ffn_gate_w, L0_ffn_w1, L0_ffn_w2, L0_ffn_w3,
                n_cores=8):
    f = np.float32
    x = np.ascontiguousarray(np.asarray(L0_attn_Z, f))
    B, S_, D_ = x.shape
    per = B // n_cores
    common = dict(
        decay=np.asarray(L0_attn_decay, f).reshape(1, 1),
        b_row=np.ascontiguousarray(np.asarray(L0_attn_b, f).reshape(1, D_)),
        gamma_row=np.ascontiguousarray(np.asarray(L0_attn_gamma, f).reshape(1, D_)),
        beta_row=np.ascontiguousarray(np.asarray(L0_attn_beta, f).reshape(1, D_)),
        ln1w_row=np.ascontiguousarray(np.asarray(L0_ffn_ln1_w, f).reshape(1, D_)),
        ln2w_row=np.ascontiguousarray(np.asarray(L0_ffn_ln2_w, f).reshape(1, D_)),
        wT=np.ascontiguousarray(np.asarray(L0_attn_W, f).T),
        wqT=np.ascontiguousarray(np.asarray(L0_ffn_wq, f).T),
        wkT=np.ascontiguousarray(np.asarray(L0_ffn_wk, f).T),
        wvT=np.ascontiguousarray(np.asarray(L0_ffn_wv, f).T),
        woT=np.ascontiguousarray(np.asarray(L0_ffn_wo, f).T),
        gateT=np.ascontiguousarray(np.asarray(L0_ffn_gate_w, f).T),
        w1T=np.ascontiguousarray(np.asarray(L0_ffn_w1, f).transpose(0, 2, 1)),
        w3T=np.ascontiguousarray(np.asarray(L0_ffn_w3, f).transpose(0, 2, 1)),
        w2T=np.ascontiguousarray(np.asarray(L0_ffn_w2, f).transpose(0, 2, 1)),
    )
    in_maps = [dict(common, x0=np.ascontiguousarray(
        x[c * per:(c + 1) * per].reshape(per * S_, D_))) for c in range(n_cores)]
    return in_maps, (B, S_, D_, per)


def kernel(**inputs):
    in_maps, (B, S_, D_, per) = host_inputs(**inputs)
    nc = build_program(T=per * S_, S=S_)
    res = run_bass_kernel_spmd(nc, in_maps, core_ids=list(range(8)))
    out = np.concatenate([res.results[c]["out"] for c in range(8)], axis=0)
    return out.reshape(B, S_, D_)


# version 04
# speedup vs baseline: 1.0333x; 1.0333x over previous
import numpy as np
import concourse.bass as bass
import concourse.mybir as mybir
from concourse.bass_utils import run_bass_kernel_spmd

F32 = mybir.dt.float32
BF16 = mybir.dt.bfloat16
AF = mybir.ActivationFunctionType
ALU = mybir.AluOpType
AX = mybir.AxisListType

D = 1024
NDC = 8
NH = 16
HD = 64
I_DIM = 2048
NIC = 16
NE = 8
EPS = 1e-5


class Ctr:
    def __init__(self, nc, name):
        self.h = nc.alloc_semaphore(name)
        self.v = 0
        self.name = name


class Prog:
    ENGINES = ("sync", "gpsimd", "tensor", "vector", "scalar")

    def __init__(self, nc):
        self.nc = nc
        self.ops = {e: [] for e in self.ENGINES}

    def ctr(self, name):
        return Ctr(self.nc, name)

    def add(self, eng, emit, waits=(), ctr=None, n=1):
        self.ops[eng].append((emit, [w for w in waits if w is not None and w[1] is not None and w[1] > 0],
                              ctr, n))
        if ctr is None:
            return None
        ctr.v += n
        return ctr.v

    def dma(self, eng, out, in_, waits=(), ctr=None, **kw):
        return self.add(eng, lambda e, o=out, i=in_, kw=kw: e.dma_start(o, i, **kw),
                        waits, ctr, 16)

    def emit_all(self):
        with self.nc.Block() as block:
            for eng in self.ENGINES:
                ops = self.ops[eng]

                def runner(engobj, ops=ops):
                    last_wait = {}
                    for emit, waits, ctr, n in ops:
                        for (c, v) in waits:
                            if last_wait.get(id(c), 0) >= v:
                                continue
                            engobj.wait_ge(c.h, v)
                            last_wait[id(c)] = v
                        instr = emit(engobj)
                        if ctr is not None:
                            instr.then_inc(ctr.h, n)
                getattr(block, eng)(runner)


def build_program(T=4096, S=2048, dbg=()):
    nseq = T // S
    STC = S // 128
    NQT = S // 512
    NKC = S // 128
    TH = T // 2
    NHALF = 2
    HTC = TH // 128
    NTT = TH // 512

    nc = bass.Bass("TRN2", target_bir_lowering=False)
    p = Prog(nc)

    IN = lambda n, s: nc.dram_tensor(n, s, F32, kind="ExternalInput").ap()
    x0_h = IN("x0", [T, D])
    decay_h = IN("decay", [1, 1])
    b_h = IN("b_row", [1, D]); gamma_h = IN("gamma_row", [1, D]); beta_h = IN("beta_row", [1, D])
    ln1w_h = IN("ln1w_row", [1, D]); ln2w_h = IN("ln2w_row", [1, D])
    wT_h = IN("wT", [D, D])
    wqT_h = IN("wqT", [D, D]); wkT_h = IN("wkT", [D, D])
    wvT_h = IN("wvT", [D, D]); woT_h = IN("woT", [D, D])
    gateT_h = IN("gateT", [D, NE])
    w1T_h = IN("w1T", [NE, D, I_DIM]); w3T_h = IN("w3T", [NE, D, I_DIM])
    w2T_h = IN("w2T", [NE, I_DIM, D])
    out_h = nc.dram_tensor("out", [T, D], F32, kind="ExternalOutput").ap()

    def scratch(name, shape, dt):
        kind = "ExternalOutput" if name in dbg else "Internal"
        return nc.dram_tensor(name, shape, dt, kind=kind).ap()

    x1_h = scratch("x1_s", [T, D], F32)
    z_h = scratch("z_s", [T, D], F32)
    f_h = scratch("f_s", [T, D], F32)
    qt_h = scratch("qt_s", [nseq, D, S], BF16)
    kt_h = scratch("kt_s", [nseq, D, S], BF16)
    v_h = scratch("v_s", [T, D], BF16)
    aot_h = scratch("aot_s", [nseq, D, S], BF16)
    r2t_h = scratch("r2t_s", [NDC, 128, T], BF16)
    we_h = scratch("we_s", [T // 128, 128, NE], F32)
    pow_h = scratch("pow_s", [1, 256], F32)
    yln_h = scratch("yln_s", [T, D], BF16)
    st_h = scratch("st_s", [T, D], BF16)
    yb_h = scratch("yb_s", [T, D], F32)

    sb = lambda n, s, d_: nc.alloc_sbuf_tensor(n, s, d_)
    b_b = sb("b_b", [128, D], F32)
    gam_b = sb("gam_b", [128, D], F32)
    bet_b = sb("bet_b", [128, D], F32)
    l1w_b = sb("l1w_b", [128, D], F32)
    l2w_b = sb("l2w_b", [128, D], F32)
    eps_c = sb("eps_c", [128, 1], F32)
    ones_bf = sb("ones_bf", [128, 128], BF16)
    ones_f = sb("ones_f", [128, 128], F32)
    ident_bf = sb("ident_bf", [128, 128], BF16)
    ident_f = sb("ident_f", [128, 128], F32)
    powx = sb("powx", [1, 256], F32)
    a_sc = sb("a_sc", [1, 1], F32)
    apow_b = sb("apow_b", [128, 128], F32)
    a128_c = sb("a128_c", [128, 1], F32)
    ltri = sb("ltri", [128, 128], BF16)
    gate_w = sb("gate_w", [128, NDC * NE], F32)
    we_all = sb("we_all", [128, (T // 128) * NE], F32)
    stat = sb("stat", [128, 96], F32)

    ra0, _ = nc.bump_sbuf(65536)
    AT = nc.alloc_sbuf_tensor_at
    wq_w = AT("wq_w", [128, NDC * D], BF16, offset=ra0)
    wk_w = AT("wk_w", [128, NDC * D], BF16, offset=ra0 + 16384)
    wv_w = AT("wv_w", [128, NDC * D], BF16, offset=ra0 + 32768)
    wo_w = AT("wo_w", [128, NDC * D], BF16, offset=ra0)
    wt_w = AT("wt_w", [128, NDC * D], BF16, offset=ra0 + 49152)
    g_t = AT("g_t", [128, NIC * TH], BF16, offset=ra0)
    rb0, _ = nc.bump_sbuf(NDC * S * 2)
    r1t_s = AT("r1t_s", [128, NDC * S], BF16, offset=rb0)
    r2in = AT("r2in", [128, NDC * TH], BF16, offset=rb0)
    rc0, _ = nc.bump_sbuf(53248)
    off = rc0
    def at(name, shape, dt):
        nonlocal off
        t = AT(name, shape, dt, offset=off)
        nbytes = int(np.prod(shape[1:])) * (2 if dt == BF16 else 4)
        off += (nbytes + 31) & ~31
        return t
    x0_t = at("x0_t", [128, 2 * D], F32)
    yln_t = at("yln_t", [128, D], BF16)
    st_t = at("st_t", [128, 2 * D], BF16)
    carry = at("carry", [128, 2 * NDC], F32)
    ybuf = at("ybuf", [128, D], F32)
    gbuf = at("gbuf", [128, D], F32)
    sqs = at("sqs", [128, D], F32)
    x1b = at("x1b", [128, 2 * D], F32)
    zb = at("zb", [128, 2 * D], F32)
    r1tmp = at("r1tmp", [128, D], F32)
    r1b = at("r1b", [128, D], BF16)
    assert off - rc0 <= 53248, off - rc0
    off = rc0
    aoin = at("aoin", [128, 2 * NDC * 128], BF16)
    zin = at("zin", [128, 2 * D], F32)
    hb = at("hb", [128, D], F32)
    r2tmp = at("r2tmp", [128, D], F32)
    r2f = at("r2f", [128, D], F32)
    r2tf = at("r2tf", [128, NDC * 128], F32)
    r2tb = at("r2tb", [128, NDC * 256], BF16)
    logit = at("logit", [128, NE], F32)
    gs1 = at("gs1", [128, NE], F32)
    gs2 = at("gs2", [128, NE], F32)
    sq4 = at("sq4", [128, D], F32)
    assert off - rc0 <= 53248, off - rc0
    off = rc0
    w2s = at("w2s", [128, NIC * D], BF16)
    w1s = at("w1s", [128, 2 * NDC * 128], BF16)
    w3s = at("w3s", [128, 2 * NDC * 128], BF16)
    silu_t = at("silu_t", [128, 2 * 512], BF16)
    ysc = at("ysc", [128, 2 * D], F32)
    assert off - rc0 <= 53248, off - rc0
    rd0, _ = nc.bump_sbuf(35712)
    off = rd0
    ybufB = at("ybufB", [128, D], F32)
    gbufB = at("gbufB", [128, D], F32)
    sqsB = at("sqsB", [128, D], F32)
    r1tmpB = at("r1tmpB", [128, D], F32)
    r1bB = AT("r1bB", [128, D], BF16, offset=rd0 + 24704)
    ylnB = AT("ylnB", [128, D], BF16, offset=rd0 + 26752)
    assert off - rd0 <= 35712, off - rd0
    off = rd0
    pstage = at("pstage", [128, 2 * 512], BF16)
    vstage = at("vstage", [128, 2 * 512], BF16)
    assert off - rd0 <= 35712, off - rd0
    off = rd0
    qt2 = at("qt2", [128, 2 * S], BF16)
    kt2 = at("kt2", [128, 2 * S], BF16)
    vh2 = at("vh2", [128, 2 * NKC * 130], BF16)
    mcols = at("mcols", [128, 32], F32)
    mrow_t = at("mrow_t", [16, 128], BF16)
    negm = at("negm", [1, 2048], BF16)
    pt_t = at("pt_t", [128, 2 * 512], BF16)
    recb = at("recb", [128, 512], F32)
    lrow = AT("lrow", [128, 512], F32, offset=off - 2048)
    ao_st = at("ao_st", [128, 512], BF16)
    assert off - rd0 <= 35712, off - rd0
    off = rd0
    fin = at("fin", [128, 2 * D], F32)
    x1in = at("x1in", [128, 2 * D], F32)
    oub = at("oub", [128, 2 * D], F32)
    sq7 = at("sq7", [128, D], F32)
    assert off - rd0 <= 35712, off - rd0

    psA = nc.alloc_psum_tensor("psA", [128, 2048], F32)
    psB = nc.alloc_psum_tensor("psB", [128, 1024], F32)
    psC = nc.alloc_psum_tensor("psC", [128, 512], F32)
    psD = nc.alloc_psum_tensor("psD", [128, 1024], BF16)

    c_ld = p.ctr("c_ld")
    c_st = p.ctr("c_st")
    c_gw = p.ctr("c_gw")
    c_ga = p.ctr("c_ga")
    c_pe = p.ctr("c_pe")
    c_dv = p.ctr("c_dv")
    c_ac = p.ctr("c_ac")

    SY, GP, TE, VE, SC = "sync", "gpsimd", "tensor", "vector", "scalar"


    def ln_stats(src, n_feat, cols, waits, sq_scratch):
        sq_scratch = sq_scratch[:]
        s0, s1, mu, var, rstd, nmr = (stat[:, cols + i:cols + i + 1] for i in range(6))
        t0 = p.add(VE, lambda e, s=src, o=s0: e.reduce_sum(o, s, axis=AX.X), waits, c_dv)
        t_sq = p.add(SC, lambda e, s=src, o=sq_scratch, a=s1: e.activation(
            o, s, AF.Square, accum_out=a), waits, c_ac)
        t1 = p.add(VE, lambda e, i=s0, o=mu: e.tensor_scalar(o, i, 1.0 / n_feat, None, op0=ALU.mult),
                   [(c_ac, t_sq), (c_dv, t0)], c_dv)
        t2 = p.add(VE, lambda e, i=mu, o=nmr: e.tensor_scalar(o, i, i, None, op0=ALU.mult),
                   [(c_dv, t1)], c_dv)
        t_var = p.add(VE, lambda e, i=s1, o=var, m=nmr: e.tensor_scalar(
            o, i, 1.0 / n_feat, m, op0=ALU.mult, op1=ALU.subtract), [(c_dv, t2)], c_dv)
        t_ln = p.add(SC, lambda e, i=var, o=rstd: e.activation(o, i, AF.Ln, bias=eps_c[:, 0:1]),
                     [(c_dv, t_var)], c_ac)
        t_ex = p.add(SC, lambda e, i=rstd, o=rstd: e.activation(o, i, AF.Exp, scale=-0.5),
                     [(c_ac, t_ln)], c_ac)
        t_nmr = p.add(VE, lambda e, i=mu, r=rstd, o=nmr: e.scalar_tensor_tensor(
            o, i, -1.0, r, op0=ALU.mult, op1=ALU.mult), [(c_ac, t_ex), (c_dv, t_var)], c_dv)
        return t_nmr, rstd, nmr

    def rms_stats(src, n_feat, cols, waits, sq_scratch):
        sq_scratch = sq_scratch[:]
        s1, msq, rr = (stat[:, cols + i:cols + i + 1] for i in range(3))
        t_sq = p.add(SC, lambda e, s=src, o=sq_scratch, a=s1: e.activation(
            o, s, AF.Square, accum_out=a), waits, c_ac)
        t_m = p.add(VE, lambda e, i=s1, o=msq: e.tensor_scalar(o, i, 1.0 / n_feat, None,
                                                               op0=ALU.mult),
                    [(c_ac, t_sq)], c_dv)
        t_ln = p.add(SC, lambda e, i=msq, o=rr: e.activation(o, i, AF.Ln, bias=eps_c[:, 0:1]),
                     [(c_dv, t_m)], c_ac)
        t_rr = p.add(SC, lambda e, i=rr, o=rr: e.activation(o, i, AF.Exp, scale=-0.5),
                     [(c_ac, t_ln)], c_ac)
        return t_rr, rr

    for dst, srch in ((b_b, b_h), (gam_b, gamma_h), (bet_b, beta_h),
                      (l1w_b, ln1w_h), (l2w_b, ln2w_h)):
        p.dma(SY, dst[:], bass.AP(srch.tensor, 0, [[0, 128], [1, D]]), (), c_ld)
    t_ld_a = p.dma(SY, a_sc[:], decay_h, (), c_ld)

    for dc in range(NDC):
        p.dma(GP, wt_w[:, dc * D:(dc + 1) * D], wT_h[dc * 128:(dc + 1) * 128, :], (), c_gw)
    for dc in range(NDC):
        t_gw_wt = p.dma(GP, gate_w[:, dc * NE:(dc + 1) * NE],
                        gateT_h[dc * 128:(dc + 1) * 128, :], (), c_gw)

    p.add(GP, lambda e: e.memset(ident_bf[:], 0.0))
    p.add(GP, lambda e: e.affine_select(out=ident_bf[:], in_=ident_bf[:],
                                        compare_op=ALU.not_equal, fill=1.0, base=0,
                                        pattern=[[-1, 128]], channel_multiplier=1))
    p.add(GP, lambda e: e.memset(ident_f[:], 0.0))
    p.add(GP, lambda e: e.affine_select(out=ident_f[:], in_=ident_f[:],
                                        compare_op=ALU.not_equal, fill=1.0, base=0,
                                        pattern=[[-1, 128]], channel_multiplier=1))
    p.add(GP, lambda e: e.memset(ones_bf[:], 1.0))
    for pb in range(2):
        for kc in range(NKC):
            for hoff in (64, 129):
                p.add(GP, lambda e, c=pb * (NKC * 130) + kc * 130 + hoff:
                      e.memset(vh2[:, c:c + 1], 1.0))
    t_gp_done = p.add(GP, lambda e: e.memset(ones_f[:], 1.0), (), c_gw)

    t_a = p.add(SC, lambda e: e.activation(a_sc[:], a_sc[:], AF.Sigmoid),
                [(c_ld, t_ld_a)], c_ac)
    p.add(VE, lambda e: e.memset(eps_c[:], EPS))

    p.add(VE, lambda e: e.memset(powx[:], 0.0), [(c_ac, t_a)])
    t_ms1 = p.add(VE, lambda e: e.memset(powx[0:1, 127:128], 1.0), (), c_dv)
    t_pow = t_ms1
    t_pow = None
    for k in range(1, 129):
        t_pow = p.add(VE, lambda e, k=k: e.tensor_scalar(
            powx[0:1, 127 + k:128 + k], powx[0:1, 126 + k:127 + k], a_sc[0:1, 0:1],
            None, op0=ALU.mult), [(c_dv, t_pow)], c_dv)
    t_pwst = p.dma(SY, pow_h, powx[0:1, :], [(c_dv, t_pow)], c_st)
    p.dma(SY, apow_b[:], bass.AP(pow_h.tensor, 128, [[0, 128], [1, 128]]),
          [(c_st, t_pwst)], c_ld)
    t_a128 = p.dma(SY, a128_c[:], bass.AP(pow_h.tensor, 255, [[0, 128], [1, 1]]), (), c_ld)
    t_ltri = None
    for srow in range(128):
        t_ltri = p.dma(GP, ltri[srow:srow + 1, :],
                       bass.AP(pow_h.tensor, 127 - srow, [[1, 1], [1, 128]]),
                       [(c_st, t_pwst)] if srow == 0 else (), c_gw)

    tok_x1_st = [0] * (T // 128)
    tok_z_st = [0] * (T // 128)
    tok_fh_st = [0] * (T // 128)
    tok_r2_st = {}
    tok_we = [0] * (T // 128)
    p4_last_pe = 0
    p4_last_r2st = 0

    for seq in range(nseq):
        w_wait = [(c_pe, p4_last_pe)] if seq > 0 else []
        t_gw_qkv = None
        for w_sbuf, w_ap in ((wq_w, wqT_h), (wk_w, wkT_h), (wv_w, wvT_h)):
            for dc in range(NDC):
                t_gw_qkv = p.dma(GP, w_sbuf[:, dc * D:(dc + 1) * D],
                                 w_ap[dc * 128:(dc + 1) * 128, :],
                                 w_wait if dc == 0 else (), c_gw)

        p1_w = [(c_pe, c_pe.v), (c_dv, c_dv.v), (c_ac, c_ac.v), (c_st, c_st.v)]
        t_carry0 = p.add(VE, lambda e: e.memset(carry[:, NDC:2 * NDC], 0.0),
                         [(c_ld, t_a128), (c_gw, t_ltri)], c_dv)

        dbg_tok = {}
        t_dv_evac = {-1: t_carry0}
        t_pe_scan = {-1: 0, -2: 0}
        t_dv_yb = {-1: 0}
        t_dv_x1add = {-1: 0, -2: 0}
        t_pe_r1t = {-1: 0, -2: 0}
        t_dv_r1t = {-1: 0, -2: 0}
        t_ac_r1s = {-1: 0}

        for tc in range(STC):
            pb = tc % 2
            gtc = seq * STC + tc
            r0 = gtc * 128
            ybuf_p = ybuf if pb == 0 else ybufB
            gbuf_p = gbuf if pb == 0 else gbufB
            sqs_p = sqs if pb == 0 else sqsB
            r1tmp_p = r1tmp if pb == 0 else r1tmpB
            r1b_p = r1b if pb == 0 else r1bB
            yln_p = yln_t if pb == 0 else ylnB
            xo = x0_t[:, pb * D:(pb + 1) * D]
            t_ldx = p.dma(SY, xo, x0_h[r0:r0 + 128, :],
                          [(c_dv, t_dv_x1add[tc - 2])] + (p1_w if tc < 2 else []), c_ld)
            lw = [(c_ld, t_ldx)]
            t_nmr, rstd, nmr = ln_stats(xo, D, 0 + 8 * pb, lw, sqs_p)
            t_yln = p.add(SC, lambda e, x=xo, r=rstd, m=nmr, o=yln_p: e.activation(
                o[:], x, AF.Identity, scale=r, bias=m),
                [(c_dv, t_nmr), (c_pe, t_pe_scan[tc - 2]),
                 (c_st, dbg_tok.get(("yln", tc - 2), 0))], c_ac)
            if "yln_s" in dbg:
                dbg_tok[("yln", tc)] = p.dma(SY, yln_h[r0:r0 + 128, :], yln_t[:],
                                             [(c_ac, t_yln)], c_st)
            t_scan = None
            for dc in range(NDC):
                t_scan = p.add(TE, lambda e, dc=dc, o=yln_p: e.matmul(
                    psA[:, dc * 128:(dc + 1) * 128], o[:, dc * 128:(dc + 1) * 128],
                    ltri[:], start=True, stop=True),
                    [(c_ac, t_yln), (c_dv, t_dv_evac[tc - 1]), (c_gw, t_ltri)]
                    if dc == 0 else (), c_pe if dc == NDC - 1 else None)
            t_pe_scan[tc] = t_scan
            po, pn = (1 - pb) * NDC, pb * NDC
            t_ev = None
            for dc in range(NDC):
                p.add(VE, lambda e, dc=dc, po=po, pn=pn: e.scalar_tensor_tensor(
                    carry[:, pn + dc:pn + dc + 1], a128_c[:], carry[:, po + dc:po + dc + 1],
                    psA[:, dc * 128 + 127:dc * 128 + 128], op0=ALU.mult, op1=ALU.add),
                    [(c_pe, t_scan)] if dc == 0 else ())
                t_ev = p.add(VE, lambda e, dc=dc, po=po, pb=pb: e.scalar_tensor_tensor(
                    st_t[:, pb * D + dc * 128:pb * D + (dc + 1) * 128], apow_b[:],
                    carry[:, po + dc:po + dc + 1], psA[:, dc * 128:(dc + 1) * 128],
                    op0=ALU.mult, op1=ALU.add),
                    [(c_st, dbg_tok.get(("st", tc - 2), 0))] if dc == 0 else (),
                    c_dv if dc == NDC - 1 else None)
            t_dv_evac[tc] = t_ev
            if "st_s" in dbg:
                dbg_tok[("st", tc)] = p.dma(SY, st_h[r0:r0 + 128, :],
                                            st_t[:, pb * D:(pb + 1) * D],
                                            [(c_dv, t_ev)], c_st)
            t_y = None
            for et in range(2):
                for dc in range(NDC):
                    t_y = p.add(TE, lambda e, et=et, dc=dc, pb=pb: e.matmul(
                        psA[:, 1024 + et * 512:1024 + (et + 1) * 512],
                        st_t[:, pb * D + dc * 128:pb * D + (dc + 1) * 128],
                        wt_w[:, dc * D + et * 512:dc * D + (et + 1) * 512],
                        start=(dc == 0), stop=(dc == NDC - 1)),
                        [(c_dv, t_ev), (c_dv, t_dv_yb[tc - 1]), (c_gw, t_gw_wt)]
                        if (et == 0 and dc == 0) else (),
                        c_pe if (et == 1 and dc == NDC - 1) else None)
            t_yb = p.add(VE, lambda e, o=ybuf_p: e.tensor_add(o[:], psA[:, 1024:2048], b_b[:]),
                         [(c_pe, t_y)], c_dv)
            if "yb_s" in dbg:
                dbg_tok[("yb", tc)] = p.dma(SY, yb_h[r0:r0 + 128, :], ybuf_p[:],
                                            [(c_dv, t_yb)], c_st)
            t_dv_yb[tc] = t_yb
            t_nmr1, rstd1, nmr1 = ln_stats(ybuf_p[:], D, 16 + 8 * pb, [(c_dv, t_yb)], sqs_p)
            t_l1 = p.add(SC, lambda e, r=rstd1, m=nmr1, i=ybuf_p, o=sqs_p: e.activation(
                o[:], i[:], AF.Identity, scale=r, bias=m), [(c_dv, t_nmr1)], c_ac)
            t_g1 = p.add(VE, lambda e, i=sqs_p, o=gbuf_p: e.tensor_mul(o[:], i[:], gam_b[:]),
                         [(c_ac, t_l1)], c_dv)
            t_g2 = p.add(VE, lambda e, o=gbuf_p: e.tensor_add(o[:], o[:], bet_b[:]),
                         [(c_dv, t_g1)], c_dv)
            t_nmr2, rstd2, nmr2 = ln_stats(gbuf_p[:], D, 32 + 8 * pb, [(c_dv, t_g2)], sqs_p)
            t_l2 = p.add(SC, lambda e, r=rstd2, m=nmr2, i=gbuf_p, o=ybuf_p: e.activation(
                o[:], i[:], AF.Identity, scale=r, bias=m),
                [(c_dv, t_nmr2), (c_st, dbg_tok.get(("yb", tc), 0))], c_ac)
            t_x1a = p.add(VE, lambda e, x=xo, pb=pb, i=ybuf_p: e.tensor_add(
                x1b[:, pb * D:(pb + 1) * D], x, i[:]),
                [(c_ac, t_l2), (c_st, tok_x1_st[gtc - 2] if gtc >= 2 else 0)], c_dv)
            t_dv_x1add[tc] = t_x1a
            tok_x1_st[gtc] = p.dma(SY, x1_h[r0:r0 + 128, :], x1b[:, pb * D:(pb + 1) * D],
                                   [(c_dv, t_x1a)], c_st)
            t_nmrz, rstdz, nmrz = ln_stats(x1b[:, pb * D:(pb + 1) * D], D, 48 + 8 * pb,
                                           [(c_dv, t_x1a)], sqs)
            t_z = p.add(SC, lambda e, r=rstdz, m=nmrz, pb=pb: e.activation(
                zb[:, pb * D:(pb + 1) * D], x1b[:, pb * D:(pb + 1) * D], AF.Identity,
                scale=r, bias=m),
                [(c_dv, t_nmrz), (c_st, tok_z_st[gtc - 2] if gtc >= 2 else 0)], c_ac)
            tok_z_st[gtc] = p.dma(SY, z_h[r0:r0 + 128, :], zb[:, pb * D:(pb + 1) * D],
                                  [(c_ac, t_z)], c_st)
            t_rr1, rr1 = rms_stats(zb[:, pb * D:(pb + 1) * D], D, 64 + 4 * pb,
                                   [(c_ac, t_z)], sqs)
            t_r1s = p.add(SC, lambda e, rr=rr1, pb=pb, o=r1tmp_p: e.activation(
                o[:], zb[:, pb * D:(pb + 1) * D], AF.Identity, scale=rr),
                [(c_ac, t_rr1), (c_dv, t_dv_r1t[tc - 2])], c_ac)
            t_ac_r1s[tc] = t_r1s
            t_r1 = p.add(VE, lambda e, i=r1tmp_p, o=r1b_p: e.tensor_mul(o[:], i[:], l1w_b[:]),
                         [(c_ac, t_r1s), (c_pe, t_pe_r1t[tc - 2])], c_dv)
            t_tr = None
            for dc in range(NDC):
                t_tr = p.add(TE, lambda e, dc=dc, i=r1b_p: e.transpose(
                    psD[:, dc * 128:(dc + 1) * 128], i[:, dc * 128:(dc + 1) * 128],
                    ident_bf[:]),
                    [(c_dv, t_r1), (c_dv, t_dv_r1t[tc - 1]), (c_gw, t_gp_done)]
                    if dc == 0 else (), c_pe if dc == NDC - 1 else None)
            t_pe_r1t[tc] = t_tr
            t_r1t = None
            for dc in range(NDC):
                t_r1t = p.add(VE, lambda e, dc=dc, tc=tc: e.tensor_copy(
                    r1t_s[:, dc * S + tc * 128:dc * S + (tc + 1) * 128],
                    psD[:, dc * 128:(dc + 1) * 128]),
                    [(c_pe, t_tr)] if dc == 0 else (), c_dv if dc == NDC - 1 else None)
            t_dv_r1t[tc] = t_r1t

        p1_last_dv = t_dv_r1t[STC - 1]

        t_p2_dv = {-1: 0, -2: 0}
        t_p2_st = {-1: 0, -2: 0}
        unit = 0
        t_mm = None
        for w_sbuf, dst_h in ((wq_w, qt_h), (wk_w, kt_h)):
            for ec in range(NDC):
                for tt4 in range(NQT):
                    pp = unit % 2
                    for dc in range(NDC):
                        t_mm = p.add(TE, lambda e, dc=dc, ec=ec, tt4=tt4, pp=pp, w=w_sbuf: e.matmul(
                            psB[:, pp * 512:(pp + 1) * 512],
                            w[:, dc * D + ec * 128:dc * D + ec * 128 + 128],
                            r1t_s[:, dc * S + tt4 * 512:dc * S + (tt4 + 1) * 512],
                            start=(dc == 0), stop=(dc == NDC - 1)),
                            [(c_dv, p1_last_dv), (c_gw, t_gw_qkv),
                             (c_dv, t_p2_dv[unit - 2])] if dc == 0 else (),
                            c_pe if dc == NDC - 1 else None)
                    t_ev2 = p.add(VE, lambda e, pp=pp: e.tensor_copy(
                        pstage[:, pp * 512:(pp + 1) * 512], psB[:, pp * 512:(pp + 1) * 512]),
                        [(c_pe, t_mm), (c_st, t_p2_st[unit - 2])], c_dv)
                    t_p2_dv[unit] = t_ev2
                    t_p2_st[unit] = p.dma(
                        SY, dst_h[seq, ec * 128:(ec + 1) * 128, tt4 * 512:(tt4 + 1) * 512],
                        pstage[:, pp * 512:(pp + 1) * 512], [(c_dv, t_ev2)], c_st)
                    unit += 1
        last_qk_st = t_p2_st[unit - 1]
        vunit = 0
        t_v_dv = {-1: 0, -2: 0}
        t_v_st = {-1: 0, -2: 0}
        for tcq in range(STC):
            for et in range(2):
                pp = vunit % 2
                for dc in range(NDC):
                    t_mm = p.add(TE, lambda e, dc=dc, tcq=tcq, et=et, pp=pp: e.matmul(
                        psA[:, pp * 512:(pp + 1) * 512],
                        r1t_s[:, dc * S + tcq * 128:dc * S + (tcq + 1) * 128],
                        wv_w[:, dc * D + et * 512:dc * D + (et + 1) * 512],
                        start=(dc == 0), stop=(dc == NDC - 1)),
                        [(c_dv, t_v_dv[vunit - 2]), (c_dv, p1_last_dv)] if dc == 0 else (),
                        c_pe if dc == NDC - 1 else None)
                t_ev3 = p.add(VE, lambda e, pp=pp: e.tensor_copy(
                    vstage[:, pp * 512:(pp + 1) * 512], psA[:, pp * 512:(pp + 1) * 512]),
                    [(c_pe, t_mm), (c_st, t_v_st[vunit - 2])], c_dv)
                t_v_dv[vunit] = t_ev3
                t_v_st[vunit] = p.dma(
                    SY, v_h[seq * S + tcq * 128:seq * S + (tcq + 1) * 128,
                            et * 512:(et + 1) * 512],
                    vstage[:, pp * 512:(pp + 1) * 512], [(c_dv, t_ev3)], c_st)
                vunit += 1
        p2_last_pe = t_mm
        last_v_st = t_v_st[vunit - 1]
        t_gw_wo = None
        for dc in range(NDC):
            t_gw_wo = p.dma(GP, wo_w[:, dc * D:(dc + 1) * D],
                            woT_h[dc * 128:(dc + 1) * 128, :],
                            [(c_pe, p2_last_pe)] if dc == 0 else (), c_gw)

        p3_dv0 = c_dv.v
        qc_glob = [0]
        t_p1max = {-1: p3_dv0, -2: p3_dv0}
        t_pair_pe_done = {-1: 0, -2: 0}
        t_ao_dv = {}
        t_recb_dv = {}
        t_st_exp = {-1: 0, -2: 0}
        t_trm_prev_dv = p3_dv0
        t_trm_tok = {-1: 0, -2: 0}
        t_aost = {-1: 0}
        pair_tok = {}
        p1_last_tok = {}
        aou = 0
        t_pv = None

        def emit_pair_loads(hp):
            pqb = hp % 2
            wv_ = [(c_pe, t_pair_pe_done[hp - 2]), (c_st, last_qk_st), (c_st, last_v_st)]
            t_lq = p.dma(SY, qt2[:, pqb * S:(pqb + 1) * S],
                         qt_h[seq, hp * 128:(hp + 1) * 128, :], wv_, c_ld)
            t_lk = p.dma(SY, kt2[:, pqb * S:(pqb + 1) * S],
                         kt_h[seq, hp * 128:(hp + 1) * 128, :], (), c_ld)
            t_lv = None
            for sub in range(2):
                h2 = 2 * hp + sub
                hoff = 0 if sub == 0 else 65
                srcv = bass.AP(v_h.tensor, (seq * S) * D + h2 * HD,
                               [[D, 128], [128 * D, NKC], [1, HD]])
                dstv = vh2[:, pqb * NKC * 130:(pqb + 1) * NKC * 130].rearrange(
                    "q (kc w) -> q kc w", kc=NKC)[:, :, hoff:hoff + HD]
                t_lv = p.dma(SY, dstv, srcv, (), c_ld)
            pair_tok[hp] = (t_lq, t_lk, t_lv)

        def make_p1(h2):
            hp = h2 // 2
            pqb = hp % 2
            hsl = slice(0, 64) if h2 % 2 == 0 else slice(64, 128)
            mbase = (h2 % 2) * 16
            state = {"i": 0}

            def emit_unit():
                i = state["i"]; state["i"] += 1
                qc, kh = i // 2, i % 2
                qg = qc_glob[0]; qc_glob[0] += 1
                pp = qg % 2
                t_s1 = None
                for kn in range(2):
                    xw = ()
                    if kn == 0:
                        xw = [(c_ld, pair_tok[hp][1]), (c_dv, t_p1max[qg - 2])]
                        if i == 0:
                            xw = xw + [(c_pe, t_trm_tok[h2 - 2])]
                    t_s1 = p.add(TE, lambda e, qc=qc, kh=kh, kn=kn, pp=pp,
                                 pqb=pqb, hsl=hsl: e.matmul(
                        psA[:, pp * 1024 + kn * 512:pp * 1024 + (kn + 1) * 512],
                        qt2[hsl, pqb * S + qc * 128:pqb * S + (qc + 1) * 128],
                        kt2[hsl, pqb * S + kh * 1024 + kn * 512:
                            pqb * S + kh * 1024 + (kn + 1) * 512],
                        start=True, stop=True), xw, c_pe if kn == 1 else None)
                if kh == 0:
                    t_mx = p.add(VE, lambda e, qc=qc, pp=pp, mbase=mbase: e.reduce_max(
                        mcols[:, mbase + qc:mbase + qc + 1],
                        psA[:, pp * 1024:(pp + 1) * 1024],
                        axis=AX.X, negate=True), [(c_pe, t_s1)], c_dv)
                else:
                    t_tmp = p.add(VE, lambda e, pp=pp: e.reduce_max(
                        stat[:, 90:91], psA[:, pp * 1024:(pp + 1) * 1024],
                        axis=AX.X, negate=True), [(c_pe, t_s1)], c_dv)
                    t_mx = p.add(VE, lambda e, qc=qc, mbase=mbase: e.tensor_tensor(
                        mcols[:, mbase + qc:mbase + qc + 1],
                        mcols[:, mbase + qc:mbase + qc + 1], stat[:, 90:91],
                        op=ALU.min), [(c_dv, t_tmp)], c_dv)
                t_p1max[qg] = t_mx
                p1_last_tok[h2] = t_mx
            return emit_unit, state

        emit_pair_loads(0)
        p1_em = {0: make_p1(0)}
        for _ in range(2 * STC):
            p1_em[0][0]()

        for h_ in range(NH):
            hp = h_ // 2
            pqb = hp % 2
            hsl = slice(0, 64) if h_ % 2 == 0 else slice(64, 128)
            hoff = 0 if h_ % 2 == 0 else 65
            if h_ % 2 == 0 and hp + 1 < NH // 2:
                emit_pair_loads(hp + 1)
            if h_ + 1 < NH:
                p1_em[h_ + 1] = make_p1(h_ + 1)
            mbase = (h_ % 2) * 16
            t_trm = p.add(TE, lambda e, mbase=mbase: e.transpose(
                psC[0:16, 0:128], mcols[:, mbase:mbase + 16], ident_f[:]),
                [(c_dv, p1_last_tok[h_]), (c_dv, t_trm_prev_dv),
                 (c_dv, t_ao_dv.get((h_ - 1, NQT - 1), p3_dv0))], c_pe)
            t_trm_tok[h_] = t_trm
            t_mr = p.add(VE, lambda e: e.tensor_copy(mrow_t[:], psC[0:16, 0:128]),
                         [(c_pe, t_trm)], c_dv)
            t_trm_prev_dv = t_mr
            t_nm = p.dma(SY, negm[0:1, :], mrow_t[:, :], [(c_dv, t_mr)], c_ld)
            for qt in range(NQT):
                pend_pv = []

                def emit_pv(kc, pp2, first_w=()):
                    nonlocal t_pv
                    pvw = [(c_ac, t_st_exp[(h_ * NQT + qt) * NKC + kc])] + list(first_w)
                    t_pv = p.add(TE, lambda e, kc=kc, pqb=pqb, hoff=hoff, pp2=pp2: e.matmul(
                        psC[0:65, 0:512],
                        vh2[:, pqb * NKC * 130 + kc * 130 + hoff:
                            pqb * NKC * 130 + kc * 130 + hoff + 65],
                        pt_t[:, pp2 * 512:(pp2 + 1) * 512],
                        start=(kc == 0), stop=(kc == NKC - 1)),
                        pvw, c_pe if kc == NKC - 1 else None)

                for kc in range(NKC):
                    stu = (h_ * NQT + qt) * NKC + kc
                    pp2 = stu % 2
                    p.add(TE, lambda e, kc=kc, qt=qt, pp2=pp2, pqb=pqb, hsl=hsl: e.matmul(
                        psB[:, pp2 * 512:(pp2 + 1) * 512],
                        kt2[hsl, pqb * S + kc * 128:pqb * S + (kc + 1) * 128],
                        qt2[hsl, pqb * S + qt * 512:pqb * S + (qt + 1) * 512],
                        start=True, stop=False),
                        [(c_ld, t_nm), (c_ac, t_st_exp[stu - 2]),
                         (c_dv, t_recb_dv.get((h_, qt - 1),
                                              t_recb_dv.get((h_ - 1, NQT - 1), p3_dv0)))])
                    t_s2 = p.add(TE, lambda e, qt=qt, pp2=pp2: e.matmul(
                        psB[:, pp2 * 512:(pp2 + 1) * 512], ones_bf[0:1, :],
                        negm[0:1, qt * 512:(qt + 1) * 512], start=False, stop=True),
                        (), c_pe)
                    t_ex = p.add(SC, lambda e, pp2=pp2: e.activation(
                        pt_t[:, pp2 * 512:(pp2 + 1) * 512],
                        psB[:, pp2 * 512:(pp2 + 1) * 512], AF.Exp, scale=0.125),
                        [(c_pe, t_s2)], c_ac)
                    t_st_exp[stu] = t_ex
                    if kc % 2 == 1 and h_ + 1 < NH and p1_em[h_ + 1][1]["i"] < 2 * STC:
                        p1_em[h_ + 1][0]()
                    if kc > 0:
                        fw = ()
                        if kc == 1:
                            fw = [(c_ld, pair_tok[hp][2]), (c_dv, t_mr),
                                  (c_dv, t_ao_dv.get((h_, qt - 1),
                                                     t_ao_dv.get((h_ - 1, NQT - 1), p3_dv0)))]
                            emit_pv(0, (h_ * NQT + qt) * NKC % 2, fw)
                        else:
                            emit_pv(kc - 1, ((h_ * NQT + qt) * NKC + kc - 1) % 2)
                emit_pv(NKC - 1, ((h_ * NQT + qt) * NKC + NKC - 1) % 2)
                t_lc = p.add(VE, lambda e: e.tensor_copy(lrow[64:65, :], psC[64:65, 0:512]),
                             [(c_pe, t_pv)], c_dv)
                t_ln2 = p.add(SC, lambda e: e.activation(lrow[64:65, :], lrow[64:65, :],
                                                         AF.Ln), [(c_dv, t_lc)], c_ac)
                t_rc = p.add(SC, lambda e: e.activation(lrow[64:65, :], lrow[64:65, :],
                                                        AF.Exp, scale=-1.0),
                             [(c_ac, t_ln2)], c_ac)
                t_bc = p.add(TE, lambda e: e.matmul(
                    psB[0:64, 0:512], ones_f[64:65, 0:64], lrow[64:65, :],
                    start=True, stop=True),
                    [(c_ac, t_rc),
                     (c_ac, t_st_exp[(h_ * NQT + qt) * NKC + NKC - 1])], c_pe)
                t_rb = p.add(VE, lambda e: e.tensor_copy(recb[0:64, :], psB[0:64, 0:512]),
                             [(c_pe, t_bc)], c_dv)
                t_recb_dv[(h_, qt)] = t_rb
                t_ao = p.add(VE, lambda e: e.tensor_mul(
                    ao_st[0:64, :], psC[0:64, 0:512], recb[0:64, :]),
                    [(c_dv, t_rb), (c_st, t_aost[aou - 1])], c_dv)
                t_ao_dv[(h_, qt)] = t_ao
                t_aost[aou] = p.dma(SY, aot_h[seq, h_ * 64:(h_ + 1) * 64,
                                              qt * 512:(qt + 1) * 512],
                                    ao_st[0:64, :], [(c_dv, t_ao)], c_st)
                aou += 1
            t_pair_pe_done[hp] = t_pv
        p3_last_st = t_aost[aou - 1]

        p4_dv0 = c_dv.v
        t_p4_dv = {-1: 0, -2: 0}
        t_r2tb_dv = {-1: 0}
        t_logit_dv = {-1: p4_dv0}
        t_hb_st = {-1: 0, -2: 0}
        t_r2_prev = 0
        t_r2s_prev = 0
        t_tr4_prev = 0
        t_gmm = None
        for tc in range(STC):
            pb = tc % 2
            gtc = seq * STC + tc
            r0 = gtc * 128
            src = bass.AP(aot_h.tensor, seq * D * S + tc * 128,
                          [[S, 128], [128 * S, NDC], [1, 128]])
            dsta = aoin[:, pb * NDC * 128:(pb + 1) * NDC * 128]
            t_lao = p.dma(SY, dsta.rearrange("q (dc w) -> q dc w", dc=NDC), src,
                          [(c_st, p3_last_st), (c_dv, t_p4_dv[tc - 2])], c_ld)
            t_lz = p.dma(SY, zin[:, pb * D:(pb + 1) * D], z_h[r0:r0 + 128, :],
                         [(c_st, tok_z_st[gtc])], c_ld)
            t_wo = None
            for et in range(2):
                for dc in range(NDC):
                    t_wo = p.add(TE, lambda e, et=et, dc=dc, pb=pb: e.matmul(
                        psA[:, pb * 1024 + et * 512:pb * 1024 + (et + 1) * 512],
                        aoin[:, pb * NDC * 128 + dc * 128:pb * NDC * 128 + (dc + 1) * 128],
                        wo_w[:, dc * D + et * 512:dc * D + (et + 1) * 512],
                        start=(dc == 0), stop=(dc == NDC - 1)),
                        [(c_ld, t_lao), (c_gw, t_gw_wo), (c_dv, t_p4_dv[tc - 2]),
                         (c_dv, p4_dv0)] if (et == 0 and dc == 0) else (),
                        c_pe if (et == 1 and dc == NDC - 1) else None)
            t_h = p.add(VE, lambda e, pb=pb: e.tensor_add(
                hb[:], psA[:, pb * 1024:(pb + 1) * 1024], zin[:, pb * D:(pb + 1) * D]),
                [(c_pe, t_wo), (c_ld, t_lz), (c_st, t_hb_st[tc - 1]),
                 (c_ac, t_r2s_prev)], c_dv)
            t_p4_dv[tc] = t_h
            t_hb_st[tc] = tok_fh_st[gtc] = p.dma(SY, f_h[r0:r0 + 128, :], hb[:],
                                                 [(c_dv, t_h)], c_st)
            t_rr2, rr2 = rms_stats(hb[:], D, 72 + 4 * pb, [(c_dv, t_h)], sq4)
            t_r2s = p.add(SC, lambda e, rr=rr2: e.activation(
                r2tmp[:], hb[:], AF.Identity, scale=rr),
                [(c_ac, t_rr2), (c_dv, t_r2_prev)], c_ac)
            t_r2s_prev = t_r2s
            t_r2 = p.add(VE, lambda e: e.tensor_mul(r2f[:], r2tmp[:], l2w_b[:]),
                         [(c_ac, t_r2s), (c_pe, t_tr4_prev)], c_dv)
            t_r2_prev = t_r2
            t_tr4 = None
            for dc in range(NDC):
                t_tr4 = p.add(TE, lambda e, dc=dc: e.transpose(
                    psB[:, dc * 128:(dc + 1) * 128], r2f[:, dc * 128:(dc + 1) * 128],
                    ident_f[:]),
                    [(c_dv, t_r2), (c_dv, t_recb_dv.get((NH - 1, NQT - 1), p4_dv0))]
                    if dc == 0 else (), c_pe if dc == NDC - 1 else None)
            t_tr4_prev = t_tr4
            for dc in range(NDC):
                p.add(VE, lambda e, dc=dc: e.tensor_copy(
                    r2tf[:, dc * 128:(dc + 1) * 128], psB[:, dc * 128:(dc + 1) * 128]),
                    [(c_pe, t_tr4)] if dc == 0 else ())
            bat = tc % 2
            t_r2tb = None
            for dc in range(NDC):
                t_r2tb = p.add(VE, lambda e, dc=dc, bat=bat: e.tensor_copy(
                    r2tb[:, dc * 256 + bat * 128:dc * 256 + (bat + 1) * 128],
                    psB[:, dc * 128:(dc + 1) * 128]),
                    [(c_st, tok_r2_st.get((gtc - 2) // 2, 0))] if dc == 0 else (),
                    c_dv if dc == NDC - 1 else None)
            t_r2tb_dv[tc] = t_r2tb
            if bat == 1:
                tok_r2 = None
                for dc in range(NDC):
                    tok_r2 = p.dma(SY, r2t_h[dc, :, (gtc - 1) * 128:(gtc + 1) * 128],
                                   r2tb[:, dc * 256:(dc + 1) * 256],
                                   [(c_dv, t_r2tb)], c_st)
                tok_r2_st[gtc // 2] = tok_r2
            for dc in range(NDC):
                t_gmm = p.add(TE, lambda e, dc=dc: e.matmul(
                    psC[:, 0:NE], r2tf[:, dc * 128:(dc + 1) * 128],
                    gate_w[:, dc * NE:(dc + 1) * NE],
                    start=(dc == 0), stop=(dc == NDC - 1)),
                    [(c_dv, t_r2tb), (c_dv, t_logit_dv[tc - 1]), (c_gw, t_gw_wt)]
                    if dc == 0 else (), c_pe if dc == NDC - 1 else None)
            g0, g1, g2, g3, g4 = (stat[:, 83 + i:84 + i] for i in range(5))
            t_lg = p.add(VE, lambda e: e.tensor_copy(logit[:], psC[:, 0:NE]),
                         [(c_pe, t_gmm)], c_dv)
            t_logit_dv[tc] = t_lg
            tg = p.add(VE, lambda e: e.reduce_max(g0, logit[:], axis=AX.X, negate=True),
                       [(c_dv, t_lg)], c_dv)
            tg = p.add(VE, lambda e: e.tensor_scalar(gs1[:], logit[:], g0, None, op0=ALU.add),
                       [(c_dv, tg)], c_dv)
            tg = p.add(VE, lambda e: e.tensor_scalar(gs2[:], gs1[:], 0.0, None, op0=ALU.is_ge),
                       [(c_dv, tg)], c_dv)
            tg = p.add(VE, lambda e: e.scalar_tensor_tensor(gs1[:], gs2[:], -1e30, logit[:],
                                                            op0=ALU.mult, op1=ALU.add),
                       [(c_dv, tg)], c_dv)
            tg = p.add(VE, lambda e: e.reduce_max(g1, gs1[:], axis=AX.X, negate=True),
                       [(c_dv, tg)], c_dv)
            t_d12 = p.add(VE, lambda e: e.tensor_sub(g2, g1, g0), [(c_dv, tg)], c_dv)
            t_sg = p.add(SC, lambda e: e.activation(g3, g2, AF.Sigmoid),
                         [(c_dv, t_d12)], c_ac)
            tg = p.add(VE, lambda e: e.tensor_scalar(g4, g3, -1.0, 1.0, op0=ALU.mult, op1=ALU.add),
                       [(c_ac, t_sg)], c_dv)
            tg = p.add(VE, lambda e: e.tensor_scalar(gs1[:], gs1[:], g1, None, op0=ALU.add),
                       [(c_dv, tg)], c_dv)
            tg = p.add(VE, lambda e: e.tensor_scalar(gs1[:], gs1[:], 0.0, None, op0=ALU.is_ge),
                       [(c_dv, tg)], c_dv)
            tg = p.add(VE, lambda e: e.tensor_scalar(gs2[:], gs2[:], g3, None, op0=ALU.mult),
                       [(c_dv, tg)], c_dv)
            t_we = p.add(VE, lambda e, gtc=gtc: e.scalar_tensor_tensor(
                we_all[:, gtc * NE:(gtc + 1) * NE], gs1[:], g4, gs2[:],
                op0=ALU.mult, op1=ALU.add), [(c_dv, tg)], c_dv)
            tok_we[gtc] = t_we
            if "we_s" in dbg:
                p.dma(SY, we_h[gtc, :, :], we_all[:, gtc * NE:(gtc + 1) * NE],
                      [(c_dv, t_we)], c_st)
        p4_last_pe = t_gmm
        p4_last_r2st = tok_r2_st[(seq * STC + STC - 1) // 2]

    p6_w = [(c_pe, c_pe.v), (c_dv, c_dv.v), (c_ac, c_ac.v), (c_st, c_st.v)]
    t_uv_dv = {-1: 0, -2: 0}
    t_y_dv = {-1: 0, -2: 0}
    t_y_pe_done = {}
    t_add = {}
    t_uv_pe_tok = {}
    last_uv_pe_of_e = {}
    t_out_st = {-1: 0, -2: 0}
    t_p7_dv = {-1: 0, -2: 0}
    uvu = 0
    yu = 0

    def prev_key(half, e_):
        if e_ > 0:
            return (half, e_ - 1)
        if half > 0:
            return (half - 1, NE - 1)
        return None

    for half in range(NHALF):
        t_r2in = None
        for dc in range(NDC):
            t_r2in = p.dma(SY, r2in[:, dc * TH:(dc + 1) * TH],
                           r2t_h[dc, :, half * TH:(half + 1) * TH],
                           ([(c_st, p4_last_r2st)] + (p6_w if half == 0 else
                            [(c_pe, last_uv_pe_of_e[(half - 1, NE - 1)])]))
                           if dc == 0 else (), c_ld)

        for e_ in range(NE):
            pk = prev_key(half, e_)
            t_w2 = None
            for ic in range(NIC):
                t_w2 = p.dma(GP, w2s[:, ic * D:(ic + 1) * D],
                             w2T_h[e_, ic * 128:(ic + 1) * 128, :],
                             ((p6_w if pk is None else [(c_pe, t_y_pe_done[pk])])
                              if ic == 0 else ()), c_gw)
            t_g = None
            t_uvmm = None
            for ic in range(NIC):
                sp = ic % 2
                w1src = bass.AP(w1T_h.tensor, e_ * D * I_DIM + ic * 128,
                                [[I_DIM, 128], [128 * I_DIM, NDC], [1, 128]])
                w3src = bass.AP(w3T_h.tensor, e_ * D * I_DIM + ic * 128,
                                [[I_DIM, 128], [128 * I_DIM, NDC], [1, 128]])
                if ic < 2:
                    wsl_wait = (p6_w if pk is None else
                                [(c_pe, t_uv_pe_tok[(pk[0], pk[1], NIC - 2 + ic)])])
                else:
                    wsl_wait = [(c_pe, t_uv_pe_tok[(half, e_, ic - 2)])]
                p.dma(GP, w1s[:, sp * 1024:(sp + 1) * 1024].rearrange(
                    "q (dc w) -> q dc w", dc=NDC), w1src, wsl_wait, c_gw)
                t_w3 = p.dma(GP, w3s[:, sp * 1024:(sp + 1) * 1024].rearrange(
                    "q (dc w) -> q dc w", dc=NDC), w3src, (), c_gw)
                for nt in range(NTT):
                    pp = uvu % 2
                    ub = psB if pp == 0 else psA
                    for dc in range(NDC):
                        uw = ()
                        if dc == 0:
                            uw = [(c_gw, t_w3), (c_ld, t_r2in), (c_dv, t_uv_dv[uvu - 2])]
                            if uvu == 0:
                                uw = uw + p6_w
                        p.add(TE, lambda e, dc=dc, nt=nt, sp=sp, ub=ub: e.matmul(
                            ub[:, 0:512],
                            w1s[:, sp * 1024 + dc * 128:sp * 1024 + (dc + 1) * 128],
                            r2in[:, dc * TH + nt * 512:dc * TH + (nt + 1) * 512],
                            start=(dc == 0), stop=(dc == NDC - 1)), uw)
                    for dc in range(NDC):
                        t_uvmm = p.add(TE, lambda e, dc=dc, nt=nt, sp=sp, ub=ub: e.matmul(
                            ub[:, 512:1024],
                            w3s[:, sp * 1024 + dc * 128:sp * 1024 + (dc + 1) * 128],
                            r2in[:, dc * TH + nt * 512:dc * TH + (nt + 1) * 512],
                            start=(dc == 0), stop=(dc == NDC - 1)),
                            (), c_pe if dc == NDC - 1 else None)
                    t_si = p.add(SC, lambda e, pp=pp, ub=ub: e.activation(
                        silu_t[:, pp * 512:(pp + 1) * 512], ub[:, 0:512], AF.Silu),
                        [(c_pe, t_uvmm)], c_ac)
                    gw_ = [(c_ac, t_si)]
                    if uvu == 0:
                        gw_ += p6_w
                    elif pk is not None and ic == 0 and nt == 0:
                        gw_ += [(c_pe, t_y_pe_done[pk])]
                    t_g = p.add(VE, lambda e, pp=pp, ub=ub, ic=ic, nt=nt: e.tensor_mul(
                        g_t[:, ic * TH + nt * 512:ic * TH + (nt + 1) * 512],
                        silu_t[:, pp * 512:(pp + 1) * 512], ub[:, 512:1024]), gw_, c_dv)
                    t_uv_dv[uvu] = t_g
                    uvu += 1
                t_uv_pe_tok[(half, e_, ic)] = t_uvmm
            last_uv_pe_of_e[(half, e_)] = t_uvmm
            t_ymm = None
            for tcl in range(HTC):
                gtc = half * HTC + tcl
                for ot in range(2):
                    yp = yu % 2
                    for ic in range(NIC):
                        t_ymm = p.add(TE, lambda e, ic=ic, tcl=tcl, ot=ot, yp=yp: e.matmul(
                            psA[:, 1024 + yp * 512:1024 + (yp + 1) * 512],
                            g_t[:, ic * TH + tcl * 128:ic * TH + tcl * 128 + 128],
                            w2s[:, ic * D + ot * 512:ic * D + (ot + 1) * 512],
                            start=(ic == 0), stop=(ic == NIC - 1)),
                            [(c_dv, t_g), (c_gw, t_w2), (c_dv, t_y_dv[yu - 2])]
                            if ic == 0 else (), c_pe if ic == NIC - 1 else None)
                    if tcl >= 2:
                        slot_prev = t_add.get((half, e_, tcl - 2), 0)
                    elif pk is not None:
                        slot_prev = t_add.get((pk[0], pk[1], HTC - 2 + tcl), 0)
                    else:
                        slot_prev = 0
                    t_ye = p.add(VE, lambda e, ot=ot, yp=yp, tcl=tcl, gtc=gtc, e_=e_:
                                 e.tensor_scalar(
                        ysc[:, (tcl % 2) * D + ot * 512:(tcl % 2) * D + (ot + 1) * 512],
                        psA[:, 1024 + yp * 512:1024 + (yp + 1) * 512],
                        we_all[:, gtc * NE + e_:gtc * NE + e_ + 1], None, op0=ALU.mult),
                        [(c_pe, t_ymm), (c_dv, tok_we[gtc]), (c_ga, slot_prev)], c_dv)
                    t_y_dv[yu] = t_ye
                    yu += 1
                r0 = gtc * 128
                order_prev = t_add.get((pk[0], pk[1], tcl), 0) if pk is not None else 0
                t_ad = p.dma(GP, f_h[r0:r0 + 128, :],
                             ysc[:, (tcl % 2) * D:(tcl % 2 + 1) * D],
                             [(c_dv, t_ye), (c_st, tok_fh_st[gtc]), (c_ga, order_prev)],
                             c_ga, accum_op=ALU.add)
                t_add[(half, e_, tcl)] = t_ad
            t_y_pe_done[(half, e_)] = t_ymm

        for tcl in range(HTC):
            gtc = half * HTC + tcl
            r0 = gtc * 128
            pb = tcl % 2
            t_lf = p.dma(SY, fin[:, pb * D:(pb + 1) * D], f_h[r0:r0 + 128, :],
                         ([(c_ga, t_add[(half, NE - 1, tcl)]),
                           (c_dv, t_p7_dv[tcl - 2])] +
                          (p6_w if (half == 0 and tcl < 2) else [])), c_ld)
            t_lx = p.dma(SY, x1in[:, pb * D:(pb + 1) * D], x1_h[r0:r0 + 128, :],
                         [(c_st, tok_x1_st[gtc])], c_ld)
            t_nmrf, rstdf, nmrf = ln_stats(fin[:, pb * D:(pb + 1) * D], D, 0 + 8 * pb,
                                           [(c_ld, t_lf)], sq7)
            t_lnf = p.add(SC, lambda e, r=rstdf, m=nmrf, pb=pb: e.activation(
                oub[:, pb * D:(pb + 1) * D], fin[:, pb * D:(pb + 1) * D], AF.Identity,
                scale=r, bias=m), [(c_dv, t_nmrf), (c_st, t_out_st[tcl - 2])], c_ac)
            t_o = p.add(VE, lambda e, pb=pb: e.tensor_add(
                oub[:, pb * D:(pb + 1) * D], oub[:, pb * D:(pb + 1) * D],
                x1in[:, pb * D:(pb + 1) * D]), [(c_ac, t_lnf), (c_ld, t_lx)], c_dv)
            t_p7_dv[tcl] = t_o
            t_out_st[tcl] = p.dma(SY, out_h[r0:r0 + 128, :], oub[:, pb * D:(pb + 1) * D],
                                  [(c_dv, t_o)], c_st)

    p.add(SY, lambda e: e.nop(), [(c_st, c_st.v), (c_ga, c_ga.v), (c_ld, c_ld.v),
                                  (c_gw, c_gw.v)])
    p.emit_all()
    return nc


def host_inputs(L0_attn_Z, L0_attn_decay, L0_attn_W, L0_attn_b, L0_attn_gamma,
                L0_attn_beta, L0_ffn_ln1_w, L0_ffn_ln2_w, L0_ffn_wq, L0_ffn_wk,
                L0_ffn_wv, L0_ffn_wo, L0_ffn_gate_w, L0_ffn_w1, L0_ffn_w2, L0_ffn_w3,
                n_cores=8):
    f = np.float32
    x = np.ascontiguousarray(np.asarray(L0_attn_Z, f))
    B, S_, D_ = x.shape
    per = B // n_cores
    common = dict(
        decay=np.asarray(L0_attn_decay, f).reshape(1, 1),
        b_row=np.ascontiguousarray(np.asarray(L0_attn_b, f).reshape(1, D_)),
        gamma_row=np.ascontiguousarray(np.asarray(L0_attn_gamma, f).reshape(1, D_)),
        beta_row=np.ascontiguousarray(np.asarray(L0_attn_beta, f).reshape(1, D_)),
        ln1w_row=np.ascontiguousarray(np.asarray(L0_ffn_ln1_w, f).reshape(1, D_)),
        ln2w_row=np.ascontiguousarray(np.asarray(L0_ffn_ln2_w, f).reshape(1, D_)),
        wT=np.ascontiguousarray(np.asarray(L0_attn_W, f).T),
        wqT=np.ascontiguousarray(np.asarray(L0_ffn_wq, f).T),
        wkT=np.ascontiguousarray(np.asarray(L0_ffn_wk, f).T),
        wvT=np.ascontiguousarray(np.asarray(L0_ffn_wv, f).T),
        woT=np.ascontiguousarray(np.asarray(L0_ffn_wo, f).T),
        gateT=np.ascontiguousarray(np.asarray(L0_ffn_gate_w, f).T),
        w1T=np.ascontiguousarray(np.asarray(L0_ffn_w1, f).transpose(0, 2, 1)),
        w3T=np.ascontiguousarray(np.asarray(L0_ffn_w3, f).transpose(0, 2, 1)),
        w2T=np.ascontiguousarray(np.asarray(L0_ffn_w2, f).transpose(0, 2, 1)),
    )
    in_maps = [dict(common, x0=np.ascontiguousarray(
        x[c * per:(c + 1) * per].reshape(per * S_, D_))) for c in range(n_cores)]
    return in_maps, (B, S_, D_, per)


def kernel(**inputs):
    in_maps, (B, S_, D_, per) = host_inputs(**inputs)
    nc = build_program(T=per * S_, S=S_)
    res = run_bass_kernel_spmd(nc, in_maps, core_ids=list(range(8)))
    out = np.concatenate([res.results[c]["out"] for c in range(8)], axis=0)
    return out.reshape(B, S_, D_)


# version 05
# speedup vs baseline: 1.0680x; 1.0336x over previous
import numpy as np
import concourse.bass as bass
import concourse.mybir as mybir
from concourse.bass_utils import run_bass_kernel_spmd

F32 = mybir.dt.float32
BF16 = mybir.dt.bfloat16
AF = mybir.ActivationFunctionType
ALU = mybir.AluOpType
AX = mybir.AxisListType

D = 1024
NDC = 8
NH = 16
HD = 64
I_DIM = 2048
NIC = 16
NE = 8
EPS = 1e-5


class Ctr:
    def __init__(self, nc, name):
        self.h = nc.alloc_semaphore(name)
        self.v = 0
        self.name = name


class Prog:
    ENGINES = ("sync", "gpsimd", "tensor", "vector", "scalar")

    def __init__(self, nc):
        self.nc = nc
        self.ops = {e: [] for e in self.ENGINES}

    def ctr(self, name):
        return Ctr(self.nc, name)

    def add(self, eng, emit, waits=(), ctr=None, n=1):
        self.ops[eng].append((emit, [w for w in waits if w is not None and w[1] is not None and w[1] > 0],
                              ctr, n))
        if ctr is None:
            return None
        ctr.v += n
        return ctr.v

    def dma(self, eng, out, in_, waits=(), ctr=None, **kw):
        return self.add(eng, lambda e, o=out, i=in_, kw=kw: e.dma_start(o, i, **kw),
                        waits, ctr, 16)

    def emit_all(self):
        with self.nc.Block() as block:
            for eng in self.ENGINES:
                ops = self.ops[eng]

                def runner(engobj, ops=ops):
                    last_wait = {}
                    for emit, waits, ctr, n in ops:
                        for (c, v) in waits:
                            if last_wait.get(id(c), 0) >= v:
                                continue
                            engobj.wait_ge(c.h, v)
                            last_wait[id(c)] = v
                        instr = emit(engobj)
                        if ctr is not None:
                            instr.then_inc(ctr.h, n)
                getattr(block, eng)(runner)


def build_program(T=4096, S=2048, dbg=()):
    nseq = T // S
    STC = S // 128
    NQT = S // 512
    NKC = S // 128
    TH = T // 2
    NHALF = 2
    HTC = TH // 128
    NTT = TH // 512

    nc = bass.Bass("TRN2", target_bir_lowering=False)
    p = Prog(nc)

    IN = lambda n, s: nc.dram_tensor(n, s, F32, kind="ExternalInput").ap()
    x0_h = IN("x0", [T, D])
    decay_h = IN("decay", [1, 1])
    b_h = IN("b_row", [1, D]); gamma_h = IN("gamma_row", [1, D]); beta_h = IN("beta_row", [1, D])
    ln1w_h = IN("ln1w_row", [1, D]); ln2w_h = IN("ln2w_row", [1, D])
    wT_h = IN("wT", [D, D])
    wqT_h = IN("wqT", [D, D]); wkT_h = IN("wkT", [D, D])
    wvT_h = IN("wvT", [D, D]); woT_h = IN("woT", [D, D])
    gateT_h = IN("gateT", [D, NE])
    w1T_h = IN("w1T", [NE, D, I_DIM]); w3T_h = IN("w3T", [NE, D, I_DIM])
    w2T_h = IN("w2T", [NE, I_DIM, D])
    out_h = nc.dram_tensor("out", [T, D], F32, kind="ExternalOutput").ap()

    def scratch(name, shape, dt):
        kind = "ExternalOutput" if name in dbg else "Internal"
        return nc.dram_tensor(name, shape, dt, kind=kind).ap()

    x1_h = scratch("x1_s", [T, D], F32)
    z_h = scratch("z_s", [T, D], F32)
    f_h = scratch("f_s", [T, D], F32)
    qt_h = scratch("qt_s", [nseq, D, S], BF16)
    kt_h = scratch("kt_s", [nseq, D, S], BF16)
    v_h = scratch("v_s", [T, D], BF16)
    aot_h = scratch("aot_s", [nseq, D, S], BF16)
    r2t_h = scratch("r2t_s", [NDC, 128, T], BF16)
    we_h = scratch("we_s", [T // 128, 128, NE], F32)
    pow_h = scratch("pow_s", [1, 256], F32)
    yln_h = scratch("yln_s", [T, D], BF16)
    st_h = scratch("st_s", [T, D], BF16)
    yb_h = scratch("yb_s", [T, D], F32)

    sb = lambda n, s, d_: nc.alloc_sbuf_tensor(n, s, d_)
    b_b = sb("b_b", [128, D], F32)
    gam_b = sb("gam_b", [128, D], F32)
    bet_b = sb("bet_b", [128, D], F32)
    l1w_b = sb("l1w_b", [128, D], F32)
    l2w_b = sb("l2w_b", [128, D], F32)
    eps_c = sb("eps_c", [128, 1], F32)
    ones_bf = sb("ones_bf", [128, 128], BF16)
    ones_f = sb("ones_f", [128, 128], F32)
    ident_bf = sb("ident_bf", [128, 128], BF16)
    ident_f = sb("ident_f", [128, 128], F32)
    powx = sb("powx", [1, 256], F32)
    a_sc = sb("a_sc", [1, 1], F32)
    apow_b = sb("apow_b", [128, 128], F32)
    a128_c = sb("a128_c", [128, 1], F32)
    ltri = sb("ltri", [128, 128], BF16)
    gate_w = sb("gate_w", [128, NDC * NE], F32)
    we_all = sb("we_all", [128, (T // 128) * NE], F32)
    stat = sb("stat", [128, 96], F32)

    ra0, _ = nc.bump_sbuf(65536)
    AT = nc.alloc_sbuf_tensor_at
    wq_w = AT("wq_w", [128, NDC * D], BF16, offset=ra0)
    wk_w = AT("wk_w", [128, NDC * D], BF16, offset=ra0 + 16384)
    wv_w = AT("wv_w", [128, NDC * D], BF16, offset=ra0 + 32768)
    wo_w = AT("wo_w", [128, NDC * D], BF16, offset=ra0)
    wt_w = AT("wt_w", [128, NDC * D], BF16, offset=ra0 + 49152)
    g_t = AT("g_t", [128, NIC * TH], BF16, offset=ra0)
    rb0, _ = nc.bump_sbuf(NDC * S * 2)
    r1t_s = AT("r1t_s", [128, NDC * S], BF16, offset=rb0)
    r2in = AT("r2in", [128, NDC * TH], BF16, offset=rb0)
    rc0, _ = nc.bump_sbuf(53248)
    off = rc0
    def at(name, shape, dt):
        nonlocal off
        t = AT(name, shape, dt, offset=off)
        nbytes = int(np.prod(shape[1:])) * (2 if dt == BF16 else 4)
        off += (nbytes + 31) & ~31
        return t
    x0_t = at("x0_t", [128, 2 * D], F32)
    yln_t = at("yln_t", [128, D], BF16)
    st_t = at("st_t", [128, 2 * D], BF16)
    carry = at("carry", [128, 2 * NDC], F32)
    ybuf = at("ybuf", [128, D], F32)
    gbuf = at("gbuf", [128, D], F32)
    sqs = at("sqs", [128, D], F32)
    x1b = at("x1b", [128, 2 * D], F32)
    zb = at("zb", [128, 2 * D], F32)
    r1tmp = at("r1tmp", [128, D], F32)
    r1b = at("r1b", [128, D], BF16)
    assert off - rc0 <= 53248, off - rc0
    off = rc0
    aoin = at("aoin", [128, 2 * NDC * 128], BF16)
    zin = at("zin", [128, 2 * D], F32)
    hb = at("hb", [128, D], F32)
    r2tmp = at("r2tmp", [128, D], F32)
    r2f = at("r2f", [128, D], F32)
    r2tf = at("r2tf", [128, NDC * 128], F32)
    r2tb = at("r2tb", [128, NDC * 256], BF16)
    logit = at("logit", [128, NE], F32)
    gs1 = at("gs1", [128, NE], F32)
    gs2 = at("gs2", [128, NE], F32)
    sq4 = at("sq4", [128, D], F32)
    assert off - rc0 <= 53248, off - rc0
    off = rc0
    w2s = at("w2s", [128, NIC * D], BF16)
    w1s = at("w1s", [128, 2 * NDC * 128], BF16)
    w3s = at("w3s", [128, 2 * NDC * 128], BF16)
    silu_t = at("silu_t", [128, 2 * 512], BF16)
    ysc = at("ysc", [128, 2 * D], F32)
    assert off - rc0 <= 53248, off - rc0
    rd0, _ = nc.bump_sbuf(35712)
    off = rd0
    ybufB = at("ybufB", [128, D], F32)
    gbufB = at("gbufB", [128, D], F32)
    sqsB = at("sqsB", [128, D], F32)
    r1tmpB = at("r1tmpB", [128, D], F32)
    r1bB = AT("r1bB", [128, D], BF16, offset=rd0 + 24704)
    ylnB = AT("ylnB", [128, D], BF16, offset=rd0 + 26752)
    assert off - rd0 <= 35712, off - rd0
    off = rd0
    pstage = at("pstage", [128, 2 * 512], BF16)
    vstage = at("vstage", [128, 2 * 512], BF16)
    assert off - rd0 <= 35712, off - rd0
    off = rd0
    qt2 = at("qt2", [128, 2 * S], BF16)
    kt2 = at("kt2", [128, 2 * S], BF16)
    vh2 = at("vh2", [128, 2 * NKC * 130], BF16)
    mcols = at("mcols", [128, 32], F32)
    mrow_t = at("mrow_t", [16, 128], BF16)
    negm = at("negm", [1, 2048], BF16)
    pt_t = at("pt_t", [128, 2 * 512], BF16)
    recb = at("recb", [128, 512], F32)
    lrow = AT("lrow", [128, 512], F32, offset=off - 2048)
    ao_st = at("ao_st", [128, 512], BF16)
    assert off - rd0 <= 35712, off - rd0
    off = rd0
    fin = at("fin", [128, 2 * D], F32)
    x1in = at("x1in", [128, 2 * D], F32)
    oub = at("oub", [128, 2 * D], F32)
    sq7 = at("sq7", [128, D], F32)
    assert off - rd0 <= 35712, off - rd0

    psA = nc.alloc_psum_tensor("psA", [128, 2048], F32)
    psB = nc.alloc_psum_tensor("psB", [128, 1024], F32)
    psC = nc.alloc_psum_tensor("psC", [128, 512], F32)
    psD = nc.alloc_psum_tensor("psD", [128, 1024], BF16)

    c_ld = p.ctr("c_ld")
    c_st = p.ctr("c_st")
    c_gw = p.ctr("c_gw")
    c_ga = p.ctr("c_ga")
    c_pe = p.ctr("c_pe")
    c_dv = p.ctr("c_dv")
    c_ac = p.ctr("c_ac")

    SY, GP, TE, VE, SC = "sync", "gpsimd", "tensor", "vector", "scalar"


    def ln_stats(src, n_feat, cols, waits, sq_scratch):
        sq_scratch = sq_scratch[:]
        s0, s1, mu, var, rstd, nmr = (stat[:, cols + i:cols + i + 1] for i in range(6))
        t0 = p.add(VE, lambda e, s=src, o=s0: e.reduce_sum(o, s, axis=AX.X), waits, c_dv)
        t_sq = p.add(SC, lambda e, s=src, o=sq_scratch, a=s1: e.activation(
            o, s, AF.Square, accum_out=a), waits, c_ac)
        t1 = p.add(VE, lambda e, i=s0, o=mu: e.tensor_scalar(o, i, 1.0 / n_feat, None, op0=ALU.mult),
                   [(c_ac, t_sq), (c_dv, t0)], c_dv)
        t2 = p.add(VE, lambda e, i=mu, o=nmr: e.tensor_scalar(o, i, i, None, op0=ALU.mult),
                   [(c_dv, t1)], c_dv)
        t_var = p.add(VE, lambda e, i=s1, o=var, m=nmr: e.tensor_scalar(
            o, i, 1.0 / n_feat, m, op0=ALU.mult, op1=ALU.subtract), [(c_dv, t2)], c_dv)
        t_ln = p.add(SC, lambda e, i=var, o=rstd: e.activation(o, i, AF.Ln, bias=eps_c[:, 0:1]),
                     [(c_dv, t_var)], c_ac)
        t_ex = p.add(SC, lambda e, i=rstd, o=rstd: e.activation(o, i, AF.Exp, scale=-0.5),
                     [(c_ac, t_ln)], c_ac)
        t_nmr = p.add(VE, lambda e, i=mu, r=rstd, o=nmr: e.scalar_tensor_tensor(
            o, i, -1.0, r, op0=ALU.mult, op1=ALU.mult), [(c_ac, t_ex), (c_dv, t_var)], c_dv)
        return t_nmr, rstd, nmr

    def rms_stats(src, n_feat, cols, waits, sq_scratch):
        sq_scratch = sq_scratch[:]
        s1, msq, rr = (stat[:, cols + i:cols + i + 1] for i in range(3))
        t_sq = p.add(SC, lambda e, s=src, o=sq_scratch, a=s1: e.activation(
            o, s, AF.Square, accum_out=a), waits, c_ac)
        t_m = p.add(VE, lambda e, i=s1, o=msq: e.tensor_scalar(o, i, 1.0 / n_feat, None,
                                                               op0=ALU.mult),
                    [(c_ac, t_sq)], c_dv)
        t_ln = p.add(SC, lambda e, i=msq, o=rr: e.activation(o, i, AF.Ln, bias=eps_c[:, 0:1]),
                     [(c_dv, t_m)], c_ac)
        t_rr = p.add(SC, lambda e, i=rr, o=rr: e.activation(o, i, AF.Exp, scale=-0.5),
                     [(c_ac, t_ln)], c_ac)
        return t_rr, rr

    for dst, srch in ((b_b, b_h), (gam_b, gamma_h), (bet_b, beta_h),
                      (l1w_b, ln1w_h), (l2w_b, ln2w_h)):
        p.dma(SY, dst[:], bass.AP(srch.tensor, 0, [[0, 128], [1, D]]), (), c_ld)
    t_ld_a = p.dma(SY, a_sc[:], decay_h, (), c_ld)

    for dc in range(NDC):
        p.dma(GP, wt_w[:, dc * D:(dc + 1) * D], wT_h[dc * 128:(dc + 1) * 128, :], (), c_gw)
    for dc in range(NDC):
        t_gw_wt = p.dma(GP, gate_w[:, dc * NE:(dc + 1) * NE],
                        gateT_h[dc * 128:(dc + 1) * 128, :], (), c_gw)

    p.add(GP, lambda e: e.memset(ident_bf[:], 0.0))
    p.add(GP, lambda e: e.affine_select(out=ident_bf[:], in_=ident_bf[:],
                                        compare_op=ALU.not_equal, fill=1.0, base=0,
                                        pattern=[[-1, 128]], channel_multiplier=1))
    p.add(GP, lambda e: e.memset(ident_f[:], 0.0))
    p.add(GP, lambda e: e.affine_select(out=ident_f[:], in_=ident_f[:],
                                        compare_op=ALU.not_equal, fill=1.0, base=0,
                                        pattern=[[-1, 128]], channel_multiplier=1))
    p.add(GP, lambda e: e.memset(ones_bf[:], 1.0))
    for pb in range(2):
        for kc in range(NKC):
            for hoff in (64, 129):
                p.add(GP, lambda e, c=pb * (NKC * 130) + kc * 130 + hoff:
                      e.memset(vh2[:, c:c + 1], 1.0))
    t_gp_done = p.add(GP, lambda e: e.memset(ones_f[:], 1.0), (), c_gw)

    t_a = p.add(SC, lambda e: e.activation(a_sc[:], a_sc[:], AF.Sigmoid),
                [(c_ld, t_ld_a)], c_ac)
    p.add(VE, lambda e: e.memset(eps_c[:], EPS))

    p.add(VE, lambda e: e.memset(powx[:], 0.0), [(c_ac, t_a)])
    t_ms1 = p.add(VE, lambda e: e.memset(powx[0:1, 127:128], 1.0), (), c_dv)
    t_pow = t_ms1
    t_pow = None
    for k in range(1, 129):
        t_pow = p.add(VE, lambda e, k=k: e.tensor_scalar(
            powx[0:1, 127 + k:128 + k], powx[0:1, 126 + k:127 + k], a_sc[0:1, 0:1],
            None, op0=ALU.mult), [(c_dv, t_pow)], c_dv)
    t_pwst = p.dma(SY, pow_h, powx[0:1, :], [(c_dv, t_pow)], c_st)
    p.dma(SY, apow_b[:], bass.AP(pow_h.tensor, 128, [[0, 128], [1, 128]]),
          [(c_st, t_pwst)], c_ld)
    t_a128 = p.dma(SY, a128_c[:], bass.AP(pow_h.tensor, 255, [[0, 128], [1, 1]]), (), c_ld)
    t_ltri = None
    for srow in range(128):
        t_ltri = p.dma(GP, ltri[srow:srow + 1, :],
                       bass.AP(pow_h.tensor, 127 - srow, [[1, 1], [1, 128]]),
                       [(c_st, t_pwst)] if srow == 0 else (), c_gw)

    tok_x1_st = [0] * (T // 128)
    tok_z_st = [0] * (T // 128)
    tok_fh_st = [0] * (T // 128)
    tok_r2_st = {}
    tok_we = [0] * (T // 128)
    p4_last_pe = 0
    p4_last_r2st = 0

    for seq in range(nseq):
        w_wait = [(c_pe, p4_last_pe)] if seq > 0 else []
        t_gw_qkv = None
        for w_sbuf, w_ap in ((wq_w, wqT_h), (wk_w, wkT_h), (wv_w, wvT_h)):
            for dc in range(NDC):
                t_gw_qkv = p.dma(GP, w_sbuf[:, dc * D:(dc + 1) * D],
                                 w_ap[dc * 128:(dc + 1) * 128, :],
                                 w_wait if dc == 0 else (), c_gw)

        p1_w = [(c_pe, c_pe.v), (c_dv, c_dv.v), (c_ac, c_ac.v), (c_st, c_st.v)]
        t_carry0 = p.add(VE, lambda e: e.memset(carry[:, NDC:2 * NDC], 0.0),
                         [(c_ld, t_a128), (c_gw, t_ltri)], c_dv)

        dbg_tok = {}
        t_dv_evac = {-1: t_carry0}
        t_pe_scan = {-1: 0, -2: 0}
        t_dv_yb = {-1: 0}
        t_dv_x1add = {-1: 0, -2: 0}
        t_pe_r1t = {-1: 0, -2: 0}
        t_dv_r1t = {-1: 0, -2: 0}
        t_ac_r1s = {-1: 0}

        for tc in range(STC):
            pb = tc % 2
            gtc = seq * STC + tc
            r0 = gtc * 128
            ybuf_p = ybuf if pb == 0 else ybufB
            gbuf_p = gbuf if pb == 0 else gbufB
            sqs_p = sqs if pb == 0 else sqsB
            r1tmp_p = r1tmp if pb == 0 else r1tmpB
            r1b_p = r1b if pb == 0 else r1bB
            yln_p = yln_t if pb == 0 else ylnB
            xo = x0_t[:, pb * D:(pb + 1) * D]
            t_ldx = p.dma(SY, xo, x0_h[r0:r0 + 128, :],
                          [(c_dv, t_dv_x1add[tc - 2])] + (p1_w if tc < 2 else []), c_ld)
            lw = [(c_ld, t_ldx)]
            t_nmr, rstd, nmr = ln_stats(xo, D, 0 + 8 * pb, lw, sqs_p)
            t_yln = p.add(SC, lambda e, x=xo, r=rstd, m=nmr, o=yln_p: e.activation(
                o[:], x, AF.Identity, scale=r, bias=m),
                [(c_dv, t_nmr), (c_pe, t_pe_scan[tc - 2]),
                 (c_st, dbg_tok.get(("yln", tc - 2), 0))], c_ac)
            if "yln_s" in dbg:
                dbg_tok[("yln", tc)] = p.dma(SY, yln_h[r0:r0 + 128, :], yln_t[:],
                                             [(c_ac, t_yln)], c_st)
            t_scan = None
            for dc in range(NDC):
                t_scan = p.add(TE, lambda e, dc=dc, o=yln_p: e.matmul(
                    psA[:, dc * 128:(dc + 1) * 128], o[:, dc * 128:(dc + 1) * 128],
                    ltri[:], start=True, stop=True),
                    [(c_ac, t_yln), (c_dv, t_dv_evac[tc - 1]), (c_gw, t_ltri)]
                    if dc == 0 else (), c_pe if dc == NDC - 1 else None)
            t_pe_scan[tc] = t_scan
            po, pn = (1 - pb) * NDC, pb * NDC
            t_ev = None
            for dc in range(NDC):
                p.add(VE, lambda e, dc=dc, po=po, pn=pn: e.scalar_tensor_tensor(
                    carry[:, pn + dc:pn + dc + 1], a128_c[:], carry[:, po + dc:po + dc + 1],
                    psA[:, dc * 128 + 127:dc * 128 + 128], op0=ALU.mult, op1=ALU.add),
                    [(c_pe, t_scan)] if dc == 0 else ())
                t_ev = p.add(VE, lambda e, dc=dc, po=po, pb=pb: e.scalar_tensor_tensor(
                    st_t[:, pb * D + dc * 128:pb * D + (dc + 1) * 128], apow_b[:],
                    carry[:, po + dc:po + dc + 1], psA[:, dc * 128:(dc + 1) * 128],
                    op0=ALU.mult, op1=ALU.add),
                    [(c_st, dbg_tok.get(("st", tc - 2), 0))] if dc == 0 else (),
                    c_dv if dc == NDC - 1 else None)
            t_dv_evac[tc] = t_ev
            if "st_s" in dbg:
                dbg_tok[("st", tc)] = p.dma(SY, st_h[r0:r0 + 128, :],
                                            st_t[:, pb * D:(pb + 1) * D],
                                            [(c_dv, t_ev)], c_st)
            t_y = None
            for et in range(2):
                for dc in range(NDC):
                    t_y = p.add(TE, lambda e, et=et, dc=dc, pb=pb: e.matmul(
                        psA[:, 1024 + et * 512:1024 + (et + 1) * 512],
                        st_t[:, pb * D + dc * 128:pb * D + (dc + 1) * 128],
                        wt_w[:, dc * D + et * 512:dc * D + (et + 1) * 512],
                        start=(dc == 0), stop=(dc == NDC - 1)),
                        [(c_dv, t_ev), (c_dv, t_dv_yb[tc - 1]), (c_gw, t_gw_wt)]
                        if (et == 0 and dc == 0) else (),
                        c_pe if (et == 1 and dc == NDC - 1) else None)
            t_yb = p.add(VE, lambda e, o=ybuf_p: e.tensor_add(o[:], psA[:, 1024:2048], b_b[:]),
                         [(c_pe, t_y)], c_dv)
            if "yb_s" in dbg:
                dbg_tok[("yb", tc)] = p.dma(SY, yb_h[r0:r0 + 128, :], ybuf_p[:],
                                            [(c_dv, t_yb)], c_st)
            t_dv_yb[tc] = t_yb
            t_nmr1, rstd1, nmr1 = ln_stats(ybuf_p[:], D, 16 + 8 * pb, [(c_dv, t_yb)], sqs_p)
            t_l1 = p.add(SC, lambda e, r=rstd1, m=nmr1, i=ybuf_p, o=sqs_p: e.activation(
                o[:], i[:], AF.Identity, scale=r, bias=m), [(c_dv, t_nmr1)], c_ac)
            t_g1 = p.add(VE, lambda e, i=sqs_p, o=gbuf_p: e.tensor_mul(o[:], i[:], gam_b[:]),
                         [(c_ac, t_l1)], c_dv)
            t_g2 = p.add(VE, lambda e, o=gbuf_p: e.tensor_add(o[:], o[:], bet_b[:]),
                         [(c_dv, t_g1)], c_dv)
            t_nmr2, rstd2, nmr2 = ln_stats(gbuf_p[:], D, 32 + 8 * pb, [(c_dv, t_g2)], sqs_p)
            t_l2 = p.add(SC, lambda e, r=rstd2, m=nmr2, i=gbuf_p, o=ybuf_p: e.activation(
                o[:], i[:], AF.Identity, scale=r, bias=m),
                [(c_dv, t_nmr2), (c_st, dbg_tok.get(("yb", tc), 0))], c_ac)
            t_x1a = p.add(VE, lambda e, x=xo, pb=pb, i=ybuf_p: e.tensor_add(
                x1b[:, pb * D:(pb + 1) * D], x, i[:]),
                [(c_ac, t_l2), (c_st, tok_x1_st[gtc - 2] if gtc >= 2 else 0)], c_dv)
            t_dv_x1add[tc] = t_x1a
            tok_x1_st[gtc] = p.dma(SY, x1_h[r0:r0 + 128, :], x1b[:, pb * D:(pb + 1) * D],
                                   [(c_dv, t_x1a)], c_st)
            t_nmrz, rstdz, nmrz = ln_stats(x1b[:, pb * D:(pb + 1) * D], D, 48 + 8 * pb,
                                           [(c_dv, t_x1a)], sqs)
            t_z = p.add(SC, lambda e, r=rstdz, m=nmrz, pb=pb: e.activation(
                zb[:, pb * D:(pb + 1) * D], x1b[:, pb * D:(pb + 1) * D], AF.Identity,
                scale=r, bias=m),
                [(c_dv, t_nmrz), (c_st, tok_z_st[gtc - 2] if gtc >= 2 else 0)], c_ac)
            tok_z_st[gtc] = p.dma(SY, z_h[r0:r0 + 128, :], zb[:, pb * D:(pb + 1) * D],
                                  [(c_ac, t_z)], c_st)
            t_rr1, rr1 = rms_stats(zb[:, pb * D:(pb + 1) * D], D, 64 + 4 * pb,
                                   [(c_ac, t_z)], sqs)
            t_r1s = p.add(SC, lambda e, rr=rr1, pb=pb, o=r1tmp_p: e.activation(
                o[:], zb[:, pb * D:(pb + 1) * D], AF.Identity, scale=rr),
                [(c_ac, t_rr1), (c_dv, t_dv_r1t[tc - 2])], c_ac)
            t_ac_r1s[tc] = t_r1s
            t_r1 = p.add(VE, lambda e, i=r1tmp_p, o=r1b_p: e.tensor_mul(o[:], i[:], l1w_b[:]),
                         [(c_ac, t_r1s), (c_pe, t_pe_r1t[tc - 2])], c_dv)
            t_tr = None
            for dc in range(NDC):
                t_tr = p.add(TE, lambda e, dc=dc, i=r1b_p: e.transpose(
                    psD[:, dc * 128:(dc + 1) * 128], i[:, dc * 128:(dc + 1) * 128],
                    ident_bf[:]),
                    [(c_dv, t_r1), (c_dv, t_dv_r1t[tc - 1]), (c_gw, t_gp_done)]
                    if dc == 0 else (), c_pe if dc == NDC - 1 else None)
            t_pe_r1t[tc] = t_tr
            t_r1t = None
            for dc in range(NDC):
                t_r1t = p.add(VE, lambda e, dc=dc, tc=tc: e.tensor_copy(
                    r1t_s[:, dc * S + tc * 128:dc * S + (tc + 1) * 128],
                    psD[:, dc * 128:(dc + 1) * 128]),
                    [(c_pe, t_tr)] if dc == 0 else (), c_dv if dc == NDC - 1 else None)
            t_dv_r1t[tc] = t_r1t

        p1_last_dv = t_dv_r1t[STC - 1]

        t_p2_dv = {-1: 0, -2: 0}
        t_p2_st = {-1: 0, -2: 0}
        unit = 0
        t_mm = None
        for w_sbuf, dst_h in ((wq_w, qt_h), (wk_w, kt_h)):
            for ec in range(NDC):
                for tt4 in range(NQT):
                    pp = unit % 2
                    for dc in range(NDC):
                        t_mm = p.add(TE, lambda e, dc=dc, ec=ec, tt4=tt4, pp=pp, w=w_sbuf: e.matmul(
                            psB[:, pp * 512:(pp + 1) * 512],
                            w[:, dc * D + ec * 128:dc * D + ec * 128 + 128],
                            r1t_s[:, dc * S + tt4 * 512:dc * S + (tt4 + 1) * 512],
                            start=(dc == 0), stop=(dc == NDC - 1)),
                            [(c_dv, p1_last_dv), (c_gw, t_gw_qkv),
                             (c_dv, t_p2_dv[unit - 2])] if dc == 0 else (),
                            c_pe if dc == NDC - 1 else None)
                    t_ev2 = p.add(VE, lambda e, pp=pp: e.tensor_copy(
                        pstage[:, pp * 512:(pp + 1) * 512], psB[:, pp * 512:(pp + 1) * 512]),
                        [(c_pe, t_mm), (c_st, t_p2_st[unit - 2])], c_dv)
                    t_p2_dv[unit] = t_ev2
                    t_p2_st[unit] = p.dma(
                        SY, dst_h[seq, ec * 128:(ec + 1) * 128, tt4 * 512:(tt4 + 1) * 512],
                        pstage[:, pp * 512:(pp + 1) * 512], [(c_dv, t_ev2)], c_st)
                    unit += 1
        last_qk_st = t_p2_st[unit - 1]
        vunit = 0
        t_v_dv = {-1: 0, -2: 0}
        t_v_st = {-1: 0, -2: 0}
        for tcq in range(STC):
            for et in range(2):
                pp = vunit % 2
                for dc in range(NDC):
                    t_mm = p.add(TE, lambda e, dc=dc, tcq=tcq, et=et, pp=pp: e.matmul(
                        psA[:, pp * 512:(pp + 1) * 512],
                        r1t_s[:, dc * S + tcq * 128:dc * S + (tcq + 1) * 128],
                        wv_w[:, dc * D + et * 512:dc * D + (et + 1) * 512],
                        start=(dc == 0), stop=(dc == NDC - 1)),
                        [(c_dv, t_v_dv[vunit - 2]), (c_dv, p1_last_dv)] if dc == 0 else (),
                        c_pe if dc == NDC - 1 else None)
                t_ev3 = p.add(VE, lambda e, pp=pp: e.tensor_copy(
                    vstage[:, pp * 512:(pp + 1) * 512], psA[:, pp * 512:(pp + 1) * 512]),
                    [(c_pe, t_mm), (c_st, t_v_st[vunit - 2])], c_dv)
                t_v_dv[vunit] = t_ev3
                t_v_st[vunit] = p.dma(
                    SY, v_h[seq * S + tcq * 128:seq * S + (tcq + 1) * 128,
                            et * 512:(et + 1) * 512],
                    vstage[:, pp * 512:(pp + 1) * 512], [(c_dv, t_ev3)], c_st)
                vunit += 1
        p2_last_pe = t_mm
        last_v_st = t_v_st[vunit - 1]
        t_gw_wo = None
        for dc in range(NDC):
            t_gw_wo = p.dma(GP, wo_w[:, dc * D:(dc + 1) * D],
                            woT_h[dc * 128:(dc + 1) * 128, :],
                            [(c_pe, p2_last_pe)] if dc == 0 else (), c_gw)

        p3_dv0 = c_dv.v
        qc_glob = [0]
        t_p1max = {-1: p3_dv0, -2: p3_dv0}
        t_pair_pe_done = {-1: 0, -2: 0}
        t_ao_dv = {}
        t_recb_dv = {}
        t_st_exp = {-1: 0, -2: 0}
        t_trm_prev_dv = p3_dv0
        t_trm_tok = {-1: 0, -2: 0}
        t_aost = {-1: 0}
        pair_tok = {}
        p1_last_tok = {}
        aou = 0
        t_pv = None

        def emit_pair_loads(hp):
            pqb = hp % 2
            wv_ = [(c_pe, t_pair_pe_done[hp - 2]), (c_st, last_qk_st), (c_st, last_v_st)]
            t_lq = p.dma(SY, qt2[:, pqb * S:(pqb + 1) * S],
                         qt_h[seq, hp * 128:(hp + 1) * 128, :], wv_, c_ld)
            t_lk = p.dma(SY, kt2[:, pqb * S:(pqb + 1) * S],
                         kt_h[seq, hp * 128:(hp + 1) * 128, :], (), c_ld)
            t_lv = None
            for sub in range(2):
                h2 = 2 * hp + sub
                hoff = 0 if sub == 0 else 65
                srcv = bass.AP(v_h.tensor, (seq * S) * D + h2 * HD,
                               [[D, 128], [128 * D, NKC], [1, HD]])
                dstv = vh2[:, pqb * NKC * 130:(pqb + 1) * NKC * 130].rearrange(
                    "q (kc w) -> q kc w", kc=NKC)[:, :, hoff:hoff + HD]
                t_lv = p.dma(SY, dstv, srcv, (), c_ld)
            pair_tok[hp] = (t_lq, t_lk, t_lv)

        def make_p1(h2):
            hp = h2 // 2
            pqb = hp % 2
            hsl = slice(0, 64) if h2 % 2 == 0 else slice(64, 128)
            mbase = (h2 % 2) * 16
            state = {"i": 0}

            def emit_unit():
                i = state["i"]; state["i"] += 1
                qc, kh = i // 2, i % 2
                qg = qc_glob[0]; qc_glob[0] += 1
                pp = qg % 2
                t_s1 = None
                for kn in range(2):
                    xw = ()
                    if kn == 0:
                        xw = [(c_ld, pair_tok[hp][1]), (c_dv, t_p1max[qg - 2])]
                        if i == 0:
                            xw = xw + [(c_pe, t_trm_tok[h2 - 2])]
                    t_s1 = p.add(TE, lambda e, qc=qc, kh=kh, kn=kn, pp=pp,
                                 pqb=pqb, hsl=hsl: e.matmul(
                        psA[:, pp * 1024 + kn * 512:pp * 1024 + (kn + 1) * 512],
                        qt2[hsl, pqb * S + qc * 128:pqb * S + (qc + 1) * 128],
                        kt2[hsl, pqb * S + kh * 1024 + kn * 512:
                            pqb * S + kh * 1024 + (kn + 1) * 512],
                        start=True, stop=True), xw, c_pe if kn == 1 else None)
                if kh == 0:
                    t_mx = p.add(VE, lambda e, qc=qc, pp=pp, mbase=mbase: e.reduce_max(
                        mcols[:, mbase + qc:mbase + qc + 1],
                        psA[:, pp * 1024:(pp + 1) * 1024],
                        axis=AX.X, negate=True), [(c_pe, t_s1)], c_dv)
                else:
                    t_tmp = p.add(VE, lambda e, pp=pp: e.reduce_max(
                        stat[:, 90:91], psA[:, pp * 1024:(pp + 1) * 1024],
                        axis=AX.X, negate=True), [(c_pe, t_s1)], c_dv)
                    t_mx = p.add(VE, lambda e, qc=qc, mbase=mbase: e.tensor_tensor(
                        mcols[:, mbase + qc:mbase + qc + 1],
                        mcols[:, mbase + qc:mbase + qc + 1], stat[:, 90:91],
                        op=ALU.min), [(c_dv, t_tmp)], c_dv)
                t_p1max[qg] = t_mx
                p1_last_tok[h2] = t_mx
            return emit_unit, state

        emit_pair_loads(0)
        p1_em = {0: make_p1(0)}
        for _ in range(2 * STC):
            p1_em[0][0]()

        for h_ in range(NH):
            hp = h_ // 2
            pqb = hp % 2
            hsl = slice(0, 64) if h_ % 2 == 0 else slice(64, 128)
            hoff = 0 if h_ % 2 == 0 else 65
            if h_ % 2 == 0 and hp + 1 < NH // 2:
                emit_pair_loads(hp + 1)
            if h_ + 1 < NH:
                p1_em[h_ + 1] = make_p1(h_ + 1)
            mbase = (h_ % 2) * 16
            t_trm = p.add(TE, lambda e, mbase=mbase: e.transpose(
                psC[0:16, 0:128], mcols[:, mbase:mbase + 16], ident_f[:]),
                [(c_dv, p1_last_tok[h_]), (c_dv, t_trm_prev_dv),
                 (c_dv, t_ao_dv.get((h_ - 1, NQT - 1), p3_dv0))], c_pe)
            t_trm_tok[h_] = t_trm
            t_mr = p.add(VE, lambda e: e.tensor_copy(mrow_t[:], psC[0:16, 0:128]),
                         [(c_pe, t_trm)], c_dv)
            t_trm_prev_dv = t_mr
            t_nm = p.dma(SY, negm[0:1, :], mrow_t[:, :], [(c_dv, t_mr)], c_ld)
            for qt in range(NQT):
                pend_pv = []

                def emit_pv(kc, pp2, first_w=()):
                    nonlocal t_pv
                    pvw = [(c_ac, t_st_exp[(h_ * NQT + qt) * NKC + kc])] + list(first_w)
                    t_pv = p.add(TE, lambda e, kc=kc, pqb=pqb, hoff=hoff, pp2=pp2: e.matmul(
                        psC[0:65, 0:512],
                        vh2[:, pqb * NKC * 130 + kc * 130 + hoff:
                            pqb * NKC * 130 + kc * 130 + hoff + 65],
                        pt_t[:, pp2 * 512:(pp2 + 1) * 512],
                        start=(kc == 0), stop=(kc == NKC - 1)),
                        pvw, c_pe if kc == NKC - 1 else None)

                for kc in range(NKC):
                    stu = (h_ * NQT + qt) * NKC + kc
                    pp2 = stu % 2
                    p.add(TE, lambda e, kc=kc, qt=qt, pp2=pp2, pqb=pqb, hsl=hsl: e.matmul(
                        psB[:, pp2 * 512:(pp2 + 1) * 512],
                        kt2[hsl, pqb * S + kc * 128:pqb * S + (kc + 1) * 128],
                        qt2[hsl, pqb * S + qt * 512:pqb * S + (qt + 1) * 512],
                        start=True, stop=False),
                        [(c_ld, t_nm), (c_ac, t_st_exp[stu - 2]),
                         (c_dv, t_recb_dv.get((h_, qt - 1),
                                              t_recb_dv.get((h_ - 1, NQT - 1), p3_dv0)))])
                    t_s2 = p.add(TE, lambda e, qt=qt, pp2=pp2: e.matmul(
                        psB[:, pp2 * 512:(pp2 + 1) * 512], ones_bf[0:1, :],
                        negm[0:1, qt * 512:(qt + 1) * 512], start=False, stop=True),
                        (), c_pe)
                    t_ex = p.add(SC, lambda e, pp2=pp2: e.activation(
                        pt_t[:, pp2 * 512:(pp2 + 1) * 512],
                        psB[:, pp2 * 512:(pp2 + 1) * 512], AF.Exp, scale=0.125),
                        [(c_pe, t_s2)], c_ac)
                    t_st_exp[stu] = t_ex
                    if kc % 2 == 1 and h_ + 1 < NH and p1_em[h_ + 1][1]["i"] < 2 * STC:
                        p1_em[h_ + 1][0]()
                    if kc > 0:
                        fw = ()
                        if kc == 1:
                            fw = [(c_ld, pair_tok[hp][2]), (c_dv, t_mr),
                                  (c_dv, t_ao_dv.get((h_, qt - 1),
                                                     t_ao_dv.get((h_ - 1, NQT - 1), p3_dv0)))]
                            emit_pv(0, (h_ * NQT + qt) * NKC % 2, fw)
                        else:
                            emit_pv(kc - 1, ((h_ * NQT + qt) * NKC + kc - 1) % 2)
                emit_pv(NKC - 1, ((h_ * NQT + qt) * NKC + NKC - 1) % 2)
                t_lc = p.add(VE, lambda e: e.tensor_copy(lrow[64:65, :], psC[64:65, 0:512]),
                             [(c_pe, t_pv)], c_dv)
                t_ln2 = p.add(SC, lambda e: e.activation(lrow[64:65, :], lrow[64:65, :],
                                                         AF.Ln), [(c_dv, t_lc)], c_ac)
                t_rc = p.add(SC, lambda e: e.activation(lrow[64:65, :], lrow[64:65, :],
                                                        AF.Exp, scale=-1.0),
                             [(c_ac, t_ln2)], c_ac)
                t_bc = p.add(TE, lambda e: e.matmul(
                    psB[0:64, 0:512], ones_f[64:65, 0:64], lrow[64:65, :],
                    start=True, stop=True),
                    [(c_ac, t_rc),
                     (c_ac, t_st_exp[(h_ * NQT + qt) * NKC + NKC - 1])], c_pe)
                t_rb = p.add(VE, lambda e: e.tensor_copy(recb[0:64, :], psB[0:64, 0:512]),
                             [(c_pe, t_bc)], c_dv)
                t_recb_dv[(h_, qt)] = t_rb
                t_ao = p.add(VE, lambda e: e.tensor_mul(
                    ao_st[0:64, :], psC[0:64, 0:512], recb[0:64, :]),
                    [(c_dv, t_rb), (c_st, t_aost[aou - 1])], c_dv)
                t_ao_dv[(h_, qt)] = t_ao
                t_aost[aou] = p.dma(SY, aot_h[seq, h_ * 64:(h_ + 1) * 64,
                                              qt * 512:(qt + 1) * 512],
                                    ao_st[0:64, :], [(c_dv, t_ao)], c_st)
                aou += 1
            t_pair_pe_done[hp] = t_pv
        p3_last_st = t_aost[aou - 1]

        p4_dv0 = c_dv.v
        t_p4_dv = {-1: 0, -2: 0}
        t_r2tb_dv = {-1: 0}
        t_logit_dv = {-1: p4_dv0}
        t_hb_st = {-1: 0, -2: 0}
        t_r2_prev = 0
        t_r2s_prev = 0
        t_tr4_prev = 0
        t_gmm = None
        for tc in range(STC):
            pb = tc % 2
            gtc = seq * STC + tc
            r0 = gtc * 128
            src = bass.AP(aot_h.tensor, seq * D * S + tc * 128,
                          [[S, 128], [128 * S, NDC], [1, 128]])
            dsta = aoin[:, pb * NDC * 128:(pb + 1) * NDC * 128]
            t_lao = p.dma(SY, dsta.rearrange("q (dc w) -> q dc w", dc=NDC), src,
                          [(c_st, p3_last_st), (c_dv, t_p4_dv[tc - 2])], c_ld)
            t_lz = p.dma(SY, zin[:, pb * D:(pb + 1) * D], z_h[r0:r0 + 128, :],
                         [(c_st, tok_z_st[gtc])], c_ld)
            t_wo = None
            for et in range(2):
                for dc in range(NDC):
                    t_wo = p.add(TE, lambda e, et=et, dc=dc, pb=pb: e.matmul(
                        psA[:, pb * 1024 + et * 512:pb * 1024 + (et + 1) * 512],
                        aoin[:, pb * NDC * 128 + dc * 128:pb * NDC * 128 + (dc + 1) * 128],
                        wo_w[:, dc * D + et * 512:dc * D + (et + 1) * 512],
                        start=(dc == 0), stop=(dc == NDC - 1)),
                        [(c_ld, t_lao), (c_gw, t_gw_wo), (c_dv, t_p4_dv[tc - 2]),
                         (c_dv, p4_dv0)] if (et == 0 and dc == 0) else (),
                        c_pe if (et == 1 and dc == NDC - 1) else None)
            t_h = p.add(VE, lambda e, pb=pb: e.tensor_add(
                hb[:], psA[:, pb * 1024:(pb + 1) * 1024], zin[:, pb * D:(pb + 1) * D]),
                [(c_pe, t_wo), (c_ld, t_lz), (c_st, t_hb_st[tc - 1]),
                 (c_ac, t_r2s_prev)], c_dv)
            t_p4_dv[tc] = t_h
            t_hb_st[tc] = tok_fh_st[gtc] = p.dma(SY, f_h[r0:r0 + 128, :], hb[:],
                                                 [(c_dv, t_h)], c_st)
            t_rr2, rr2 = rms_stats(hb[:], D, 72 + 4 * pb, [(c_dv, t_h)], sq4)
            t_r2s = p.add(SC, lambda e, rr=rr2: e.activation(
                r2tmp[:], hb[:], AF.Identity, scale=rr),
                [(c_ac, t_rr2), (c_dv, t_r2_prev)], c_ac)
            t_r2s_prev = t_r2s
            t_r2 = p.add(VE, lambda e: e.tensor_mul(r2f[:], r2tmp[:], l2w_b[:]),
                         [(c_ac, t_r2s), (c_pe, t_tr4_prev)], c_dv)
            t_r2_prev = t_r2
            t_tr4 = None
            for dc in range(NDC):
                t_tr4 = p.add(TE, lambda e, dc=dc: e.transpose(
                    psB[:, dc * 128:(dc + 1) * 128], r2f[:, dc * 128:(dc + 1) * 128],
                    ident_f[:]),
                    [(c_dv, t_r2), (c_dv, t_recb_dv.get((NH - 1, NQT - 1), p4_dv0))]
                    if dc == 0 else (), c_pe if dc == NDC - 1 else None)
            t_tr4_prev = t_tr4
            for dc in range(NDC):
                p.add(VE, lambda e, dc=dc: e.tensor_copy(
                    r2tf[:, dc * 128:(dc + 1) * 128], psB[:, dc * 128:(dc + 1) * 128]),
                    [(c_pe, t_tr4)] if dc == 0 else ())
            bat = tc % 2
            t_r2tb = None
            for dc in range(NDC):
                t_r2tb = p.add(VE, lambda e, dc=dc, bat=bat: e.tensor_copy(
                    r2tb[:, dc * 256 + bat * 128:dc * 256 + (bat + 1) * 128],
                    psB[:, dc * 128:(dc + 1) * 128]),
                    [(c_st, tok_r2_st.get((gtc - 2) // 2, 0))] if dc == 0 else (),
                    c_dv if dc == NDC - 1 else None)
            t_r2tb_dv[tc] = t_r2tb
            if bat == 1:
                tok_r2 = None
                for dc in range(NDC):
                    tok_r2 = p.dma(SY, r2t_h[dc, :, (gtc - 1) * 128:(gtc + 1) * 128],
                                   r2tb[:, dc * 256:(dc + 1) * 256],
                                   [(c_dv, t_r2tb)], c_st)
                tok_r2_st[gtc // 2] = tok_r2
            for dc in range(NDC):
                t_gmm = p.add(TE, lambda e, dc=dc: e.matmul(
                    psC[:, 0:NE], r2tf[:, dc * 128:(dc + 1) * 128],
                    gate_w[:, dc * NE:(dc + 1) * NE],
                    start=(dc == 0), stop=(dc == NDC - 1)),
                    [(c_dv, t_r2tb), (c_dv, t_logit_dv[tc - 1]), (c_gw, t_gw_wt)]
                    if dc == 0 else (), c_pe if dc == NDC - 1 else None)
            g0, g1, g2, g3, g4 = (stat[:, 83 + i:84 + i] for i in range(5))
            t_lg = p.add(VE, lambda e: e.tensor_copy(logit[:], psC[:, 0:NE]),
                         [(c_pe, t_gmm)], c_dv)
            t_logit_dv[tc] = t_lg
            tg = p.add(VE, lambda e: e.reduce_max(g0, logit[:], axis=AX.X, negate=True),
                       [(c_dv, t_lg)], c_dv)
            tg = p.add(VE, lambda e: e.tensor_scalar(gs1[:], logit[:], g0, None, op0=ALU.add),
                       [(c_dv, tg)], c_dv)
            tg = p.add(VE, lambda e: e.tensor_scalar(gs2[:], gs1[:], 0.0, None, op0=ALU.is_ge),
                       [(c_dv, tg)], c_dv)
            tg = p.add(VE, lambda e: e.scalar_tensor_tensor(gs1[:], gs2[:], -1e30, logit[:],
                                                            op0=ALU.mult, op1=ALU.add),
                       [(c_dv, tg)], c_dv)
            tg = p.add(VE, lambda e: e.reduce_max(g1, gs1[:], axis=AX.X, negate=True),
                       [(c_dv, tg)], c_dv)
            t_d12 = p.add(VE, lambda e: e.tensor_sub(g2, g1, g0), [(c_dv, tg)], c_dv)
            t_sg = p.add(SC, lambda e: e.activation(g3, g2, AF.Sigmoid),
                         [(c_dv, t_d12)], c_ac)
            tg = p.add(VE, lambda e: e.tensor_scalar(g4, g3, -1.0, 1.0, op0=ALU.mult, op1=ALU.add),
                       [(c_ac, t_sg)], c_dv)
            tg = p.add(VE, lambda e: e.tensor_scalar(gs1[:], gs1[:], g1, None, op0=ALU.add),
                       [(c_dv, tg)], c_dv)
            tg = p.add(VE, lambda e: e.tensor_scalar(gs1[:], gs1[:], 0.0, None, op0=ALU.is_ge),
                       [(c_dv, tg)], c_dv)
            tg = p.add(VE, lambda e: e.tensor_scalar(gs2[:], gs2[:], g3, None, op0=ALU.mult),
                       [(c_dv, tg)], c_dv)
            t_we = p.add(VE, lambda e, gtc=gtc: e.scalar_tensor_tensor(
                we_all[:, gtc * NE:(gtc + 1) * NE], gs1[:], g4, gs2[:],
                op0=ALU.mult, op1=ALU.add), [(c_dv, tg)], c_dv)
            tok_we[gtc] = t_we
            if "we_s" in dbg:
                p.dma(SY, we_h[gtc, :, :], we_all[:, gtc * NE:(gtc + 1) * NE],
                      [(c_dv, t_we)], c_st)
        p4_last_pe = t_gmm
        p4_last_r2st = tok_r2_st[(seq * STC + STC - 1) // 2]

    p6_w = [(c_pe, c_pe.v), (c_dv, c_dv.v), (c_ac, c_ac.v), (c_st, c_st.v)]
    t_uv_dv = {-1: 0, -2: 0}
    t_y_dv = {-1: 0, -2: 0}
    t_slice_ld = {}
    t_y_pe_done = {}
    t_add = {}
    t_uv_pe_tok = {}
    last_uv_pe_of_e = {}
    t_out_st = {-1: 0, -2: 0}
    t_p7_dv = {-1: 0, -2: 0}
    uvu = 0
    yu = 0

    def prev_key(half, e_):
        if e_ > 0:
            return (half, e_ - 1)
        if half > 0:
            return (half - 1, NE - 1)
        return None

    def emit_p7(half2, tcl):
        gtc = half2 * HTC + tcl
        r0 = gtc * 128
        pb = tcl % 2
        k7 = half2 * HTC + tcl
        t_lf = p.dma(SY, fin[:, pb * D:(pb + 1) * D], f_h[r0:r0 + 128, :],
                     ([(c_ga, t_add[(half2, NE - 1, tcl)]),
                       (c_dv, t_p7_dv[k7 - 2])] +
                      (p6_w if (half2 == 0 and tcl < 2) else [])), c_ld)
        t_lx = p.dma(SY, x1in[:, pb * D:(pb + 1) * D], x1_h[r0:r0 + 128, :],
                     [(c_st, tok_x1_st[gtc])], c_ld)
        t_nmrf, rstdf, nmrf = ln_stats(fin[:, pb * D:(pb + 1) * D], D, 0 + 8 * pb,
                                       [(c_ld, t_lf)], sq7)
        t_lnf = p.add(SC, lambda e, r=rstdf, m=nmrf, pb=pb: e.activation(
            oub[:, pb * D:(pb + 1) * D], fin[:, pb * D:(pb + 1) * D], AF.Identity,
            scale=r, bias=m), [(c_dv, t_nmrf), (c_st, t_out_st[k7 - 2])], c_ac)
        t_o = p.add(VE, lambda e, pb=pb: e.tensor_add(
            oub[:, pb * D:(pb + 1) * D], oub[:, pb * D:(pb + 1) * D],
            x1in[:, pb * D:(pb + 1) * D]), [(c_ac, t_lnf), (c_ld, t_lx)], c_dv)
        t_p7_dv[k7] = t_o
        t_out_st[k7] = p.dma(SY, out_h[r0:r0 + 128, :], oub[:, pb * D:(pb + 1) * D],
                             [(c_dv, t_o)], c_st)

    for half in range(NHALF):
        t_r2in = None
        for dc in range(NDC):
            t_r2in = p.dma(SY, r2in[:, dc * TH:(dc + 1) * TH],
                           r2t_h[dc, :, half * TH:(half + 1) * TH],
                           ([(c_st, p4_last_r2st)] + (p6_w if half == 0 else
                            [(c_pe, last_uv_pe_of_e[(half - 1, NE - 1)])]))
                           if dc == 0 else (), c_ld)

        for e_ in range(NE):
            pk = prev_key(half, e_)

            def emit_slice_load(e2, ic, wsl_wait):
                sp = ic % 2
                w1src = bass.AP(w1T_h.tensor, e2 * D * I_DIM + ic * 128,
                                [[I_DIM, 128], [128 * I_DIM, NDC], [1, 128]])
                w3src = bass.AP(w3T_h.tensor, e2 * D * I_DIM + ic * 128,
                                [[I_DIM, 128], [128 * I_DIM, NDC], [1, 128]])
                p.dma(GP, w1s[:, sp * 1024:(sp + 1) * 1024].rearrange(
                    "q (dc w) -> q dc w", dc=NDC), w1src, wsl_wait, c_gw)
                return p.dma(GP, w3s[:, sp * 1024:(sp + 1) * 1024].rearrange(
                    "q (dc w) -> q dc w", dc=NDC), w3src, (), c_gw)

            if (half, e_, 0) not in t_slice_ld:
                for icx in range(2):
                    wsl = (p6_w if pk is None else
                           [(c_pe, t_uv_pe_tok[(pk[0], pk[1], NIC - 2 + icx)])])
                    t_slice_ld[(half, e_, icx)] = emit_slice_load(e_, icx, wsl)
            t_w2 = None
            for ic in range(NIC):
                t_w2 = p.dma(GP, w2s[:, ic * D:(ic + 1) * D],
                             w2T_h[e_, ic * 128:(ic + 1) * 128, :],
                             ((p6_w if pk is None else [(c_pe, t_y_pe_done[pk])])
                              if ic == 0 else ()), c_gw)
            t_g = None
            t_uvmm = None
            for ic in range(NIC):
                sp = ic % 2
                if (half, e_, ic) in t_slice_ld:
                    t_w3 = t_slice_ld[(half, e_, ic)]
                else:
                    t_w3 = emit_slice_load(
                        e_, ic, [(c_pe, t_uv_pe_tok[(half, e_, ic - 2)])])
                for nt in range(NTT):
                    pp = uvu % 2
                    ub = psB if pp == 0 else psA
                    for dc in range(NDC):
                        uw = ()
                        if dc == 0:
                            uw = [(c_gw, t_w3), (c_ld, t_r2in), (c_dv, t_uv_dv[uvu - 2])]
                            if uvu == 0:
                                uw = uw + p6_w
                        p.add(TE, lambda e, dc=dc, nt=nt, sp=sp, ub=ub: e.matmul(
                            ub[:, 0:512],
                            w1s[:, sp * 1024 + dc * 128:sp * 1024 + (dc + 1) * 128],
                            r2in[:, dc * TH + nt * 512:dc * TH + (nt + 1) * 512],
                            start=(dc == 0), stop=(dc == NDC - 1)), uw)
                    for dc in range(NDC):
                        t_uvmm = p.add(TE, lambda e, dc=dc, nt=nt, sp=sp, ub=ub: e.matmul(
                            ub[:, 512:1024],
                            w3s[:, sp * 1024 + dc * 128:sp * 1024 + (dc + 1) * 128],
                            r2in[:, dc * TH + nt * 512:dc * TH + (nt + 1) * 512],
                            start=(dc == 0), stop=(dc == NDC - 1)),
                            (), c_pe if dc == NDC - 1 else None)
                    t_si = p.add(SC, lambda e, pp=pp, ub=ub: e.activation(
                        silu_t[:, pp * 512:(pp + 1) * 512], ub[:, 0:512], AF.Silu),
                        [(c_pe, t_uvmm)], c_ac)
                    gw_ = [(c_ac, t_si)]
                    if uvu == 0:
                        gw_ += p6_w
                    elif pk is not None and ic == 0 and nt == 0:
                        gw_ += [(c_pe, t_y_pe_done[pk])]
                    t_g = p.add(VE, lambda e, pp=pp, ub=ub, ic=ic, nt=nt: e.tensor_mul(
                        g_t[:, ic * TH + nt * 512:ic * TH + (nt + 1) * 512],
                        silu_t[:, pp * 512:(pp + 1) * 512], ub[:, 512:1024]), gw_, c_dv)
                    t_uv_dv[uvu] = t_g
                    uvu += 1
                t_uv_pe_tok[(half, e_, ic)] = t_uvmm
                if half > 0 and e_ == 0:
                    emit_p7(half - 1, ic)
            last_uv_pe_of_e[(half, e_)] = t_uvmm
            nk = (half, e_ + 1) if e_ + 1 < NE else (
                (half + 1, 0) if half + 1 < NHALF else None)
            if nk is not None:
                for icx in range(2):
                    t_slice_ld[(nk[0], nk[1], icx)] = emit_slice_load(
                        nk[1], icx, [(c_pe, t_uv_pe_tok[(half, e_, NIC - 2 + icx)])])
            t_ymm = None
            for tcl in range(HTC):
                gtc = half * HTC + tcl
                for ot in range(2):
                    yp = yu % 2
                    for ic in range(NIC):
                        t_ymm = p.add(TE, lambda e, ic=ic, tcl=tcl, ot=ot, yp=yp: e.matmul(
                            psA[:, 1024 + yp * 512:1024 + (yp + 1) * 512],
                            g_t[:, ic * TH + tcl * 128:ic * TH + tcl * 128 + 128],
                            w2s[:, ic * D + ot * 512:ic * D + (ot + 1) * 512],
                            start=(ic == 0), stop=(ic == NIC - 1)),
                            [(c_dv, t_g), (c_gw, t_w2), (c_dv, t_y_dv[yu - 2])]
                            if ic == 0 else (), c_pe if ic == NIC - 1 else None)
                    if tcl >= 2:
                        slot_prev = t_add.get((half, e_, tcl - 2), 0)
                    elif pk is not None:
                        slot_prev = t_add.get((pk[0], pk[1], HTC - 2 + tcl), 0)
                    else:
                        slot_prev = 0
                    t_ye = p.add(VE, lambda e, ot=ot, yp=yp, tcl=tcl, gtc=gtc, e_=e_:
                                 e.tensor_scalar(
                        ysc[:, (tcl % 2) * D + ot * 512:(tcl % 2) * D + (ot + 1) * 512],
                        psA[:, 1024 + yp * 512:1024 + (yp + 1) * 512],
                        we_all[:, gtc * NE + e_:gtc * NE + e_ + 1], None, op0=ALU.mult),
                        [(c_pe, t_ymm), (c_dv, tok_we[gtc]), (c_ga, slot_prev)], c_dv)
                    t_y_dv[yu] = t_ye
                    yu += 1
                r0 = gtc * 128
                order_prev = t_add.get((pk[0], pk[1], tcl), 0) if pk is not None else 0
                t_ad = p.dma(GP, f_h[r0:r0 + 128, :],
                             ysc[:, (tcl % 2) * D:(tcl % 2 + 1) * D],
                             [(c_dv, t_ye), (c_st, tok_fh_st[gtc]), (c_ga, order_prev)],
                             c_ga, accum_op=ALU.add)
                t_add[(half, e_, tcl)] = t_ad
            t_y_pe_done[(half, e_)] = t_ymm

        if half == NHALF - 1:
            for tcl in range(HTC):
                emit_p7(half, tcl)

    p.add(SY, lambda e: e.nop(), [(c_st, c_st.v), (c_ga, c_ga.v), (c_ld, c_ld.v),
                                  (c_gw, c_gw.v)])
    p.emit_all()
    return nc


def host_inputs(L0_attn_Z, L0_attn_decay, L0_attn_W, L0_attn_b, L0_attn_gamma,
                L0_attn_beta, L0_ffn_ln1_w, L0_ffn_ln2_w, L0_ffn_wq, L0_ffn_wk,
                L0_ffn_wv, L0_ffn_wo, L0_ffn_gate_w, L0_ffn_w1, L0_ffn_w2, L0_ffn_w3,
                n_cores=8):
    f = np.float32
    x = np.ascontiguousarray(np.asarray(L0_attn_Z, f))
    B, S_, D_ = x.shape
    per = B // n_cores
    common = dict(
        decay=np.asarray(L0_attn_decay, f).reshape(1, 1),
        b_row=np.ascontiguousarray(np.asarray(L0_attn_b, f).reshape(1, D_)),
        gamma_row=np.ascontiguousarray(np.asarray(L0_attn_gamma, f).reshape(1, D_)),
        beta_row=np.ascontiguousarray(np.asarray(L0_attn_beta, f).reshape(1, D_)),
        ln1w_row=np.ascontiguousarray(np.asarray(L0_ffn_ln1_w, f).reshape(1, D_)),
        ln2w_row=np.ascontiguousarray(np.asarray(L0_ffn_ln2_w, f).reshape(1, D_)),
        wT=np.ascontiguousarray(np.asarray(L0_attn_W, f).T),
        wqT=np.ascontiguousarray(np.asarray(L0_ffn_wq, f).T),
        wkT=np.ascontiguousarray(np.asarray(L0_ffn_wk, f).T),
        wvT=np.ascontiguousarray(np.asarray(L0_ffn_wv, f).T),
        woT=np.ascontiguousarray(np.asarray(L0_ffn_wo, f).T),
        gateT=np.ascontiguousarray(np.asarray(L0_ffn_gate_w, f).T),
        w1T=np.ascontiguousarray(np.asarray(L0_ffn_w1, f).transpose(0, 2, 1)),
        w3T=np.ascontiguousarray(np.asarray(L0_ffn_w3, f).transpose(0, 2, 1)),
        w2T=np.ascontiguousarray(np.asarray(L0_ffn_w2, f).transpose(0, 2, 1)),
    )
    in_maps = [dict(common, x0=np.ascontiguousarray(
        x[c * per:(c + 1) * per].reshape(per * S_, D_))) for c in range(n_cores)]
    return in_maps, (B, S_, D_, per)


def kernel(**inputs):
    in_maps, (B, S_, D_, per) = host_inputs(**inputs)
    nc = build_program(T=per * S_, S=S_)
    res = run_bass_kernel_spmd(nc, in_maps, core_ids=list(range(8)))
    out = np.concatenate([res.results[c]["out"] for c in range(8)], axis=0)
    return out.reshape(B, S_, D_)
